# version 08
# baseline (speedup 1.0000x reference)
import numpy as np
import ml_dtypes
from contextlib import ExitStack
import concourse.bass as bass
import concourse.mybir as mybir
from concourse.bass_utils import run_bass_kernel_spmd

F32 = mybir.dt.float32
BF16 = mybir.dt.bfloat16
AF = mybir.ActivationFunctionType
ALU = mybir.AluOpType
AX = mybir.AxisListType.X
bf16 = ml_dtypes.bfloat16

NCORE = 8
TC = 4096
D = 1024
NH = 16
DH = 64
CH = 64
NBAND = 8
BTB = 256
BT = 512
TPB = 4
CPB = 4

_DBG = {"upto": None, "outs": []}


class Em:
    ENG = dict(t="tensor", v="vector", s="scalar", y="sync", g="gpsimd")

    def __init__(self, nc, block, ctx):
        self.nc = nc
        self.block = block
        self.sem = {k: ctx.enter_context(nc.semaphore("sem_" + k)) for k in self.ENG}
        self.cnt = {k: 0 for k in self.ENG}
        self.lastw = {}

    def stage(self, ek, fns, waits=None, dma=False, selfdep=False, noinc=False):
        sem = self.sem
        lw = self.lastw
        wl = [(k, v) for k, v in (waits or {}).items() if k != ek and v > 0]
        own = self.cnt[ek]

        serial_dma = dma and ek == "g"
        def body(e):
            if (selfdep or dma) and own > 0 and lw.get((ek, ek), 0) < own:
                e.wait_ge(sem[ek], own)
                lw[(ek, ek)] = own
            for k, v in wl:
                if lw.get((ek, k), 0) >= v:
                    continue
                e.wait_ge(sem[k], v)
                lw[(ek, k)] = v
            run = own
            for i, f in enumerate(fns):
                if serial_dma and i > 0:
                    e.wait_ge(sem[ek], run)
                    lw[(ek, ek)] = run
                ins = f(e)
                if not noinc:
                    ins.then_inc(sem[ek], 16 if dma else 1)
                run += 16 if dma else 1

        getattr(self.block, self.ENG[ek])(body)
        if not noinc:
            self.cnt[ek] += (16 if dma else 1) * len(fns)
        return self.cnt[ek]

    def snap(self):
        return dict(self.cnt)

    def barrier(self):
        c = self.snap()
        for ek in self.ENG:
            w = {k: v for k, v in c.items() if k != ek}
            self.stage(ek, [lambda e: e.nop()], waits=w, noinc=True)


def build_nc(dbg_upto=None):
    nc = bass.Bass("TRN2", target_bir_lowering=False)
    ap = {}
    def din(name, shape, dt=F32):
        ap[name] = nc.dram_tensor(name, shape, dt, kind="ExternalInput").ap()
        return ap[name]

    x_h = din("x", [TC, D])
    wq_h = din("wqT", [D, D])
    wk_h = din("wkT", [D, D])
    wv_h = din("wvT", [D, D])
    wo_h = din("woT", [D, D])
    ctrl_h = din("ctrl", [D, 64])
    f1_h = din("f1w", [D, 4096])
    f2_h = din("f2w", [4096, D])
    biasT_h = din("biasT", [64, 64])
    araw_h = din("araw", [1, 16])
    braw_h = din("braw", [1, 16])
    ident_h = din("ident", [128, 128])
    maskj_h = din("maskj", [64, 64])
    masklo_h = din("masklo", [64, 64])
    maskup_h = din("maskup", [64, 64])
    tie_h = din("tie", [1, 128])
    gmask_h = din("gmask", [128, 8])

    out_h = nc.dram_tensor("out", [TC, D], F32, kind="ExternalOutput").ap()
    x1_d = nc.dram_tensor("x1d", [TC, D], BF16, kind="Internal").ap()
    inter_d = nc.dram_tensor("interd", [64 * 256, D], BF16, kind="Internal").ap()

    dbg_outs = []
    def dbg(name, shape, dt=BF16):
        a = nc.dram_tensor(name, shape, dt, kind="ExternalOutput").ap()
        dbg_outs.append(name)
        return a

    with ExitStack() as ctx:
        em_ctx = ctx
        ident_bf = ctx.enter_context(nc.sbuf_tensor("ident_bf", [128, 128], BF16))
        maskj_bf = ctx.enter_context(nc.sbuf_tensor("maskj_bf", [128, 64], BF16))
        mlo128 = ctx.enter_context(nc.sbuf_tensor("mlo128", [128, 64], F32))
        mup128 = ctx.enter_context(nc.sbuf_tensor("mup128", [128, 64], F32))
        mA0 = ctx.enter_context(nc.sbuf_tensor("mA0", [128, 1024], BF16))
        mB0 = ctx.enter_context(nc.sbuf_tensor("mB0", [128, 1024], BF16))
        abrow = ctx.enter_context(nc.sbuf_tensor("abrow", [1, 128], F32))
        abbc = ctx.enter_context(nc.sbuf_tensor("abbc", [128, 48], F32))
        ones1 = ctx.enter_context(nc.sbuf_tensor("ones1", [1, 128], F32))
        cst = ctx.enter_context(nc.sbuf_tensor("cst", [128, 4], F32))
        onescol = ctx.enter_context(nc.sbuf_tensor("onescol", [128, 1], BF16))
        tie_sb = ctx.enter_context(nc.sbuf_tensor("tie_sb", [1, 128], F32))
        ctrl_sb = ctx.enter_context(nc.sbuf_tensor("ctrl_sb", [128, 512], BF16))
        biasT_sb = ctx.enter_context(nc.sbuf_tensor("biasT_sb", [64, 64], F32))
        gmask_sb = ctx.enter_context(nc.sbuf_tensor("gmask_sb", [128, 8], BF16))

        block = ctx.enter_context(nc.Block())
        em = Em(nc, block, em_ctx)
        E = em.stage

        init_ctx = ExitStack()
        psIN = init_ctx.enter_context(nc.psum_tensor("psIN", [128, 512], F32))

        E("g", [
            lambda g: g.dma_start(out=ident_bf[:], in_=ident_h),
            lambda g: g.dma_start(out=maskj_bf[0:64, :], in_=maskj_h),
            lambda g: g.dma_start(out=maskj_bf[64:128, :], in_=maskj_h),
            lambda g: g.dma_start(out=mlo128[0:64, :], in_=masklo_h),
            lambda g: g.dma_start(out=mlo128[64:128, :], in_=masklo_h),
            lambda g: g.dma_start(out=mup128[0:64, :], in_=maskup_h),
            lambda g: g.dma_start(out=mup128[64:128, :], in_=maskup_h),
            lambda g: g.dma_start(out=abrow[:, 0:16], in_=araw_h),
            lambda g: g.dma_start(out=abrow[:, 16:32], in_=braw_h),
            lambda g: g.dma_start(out=tie_sb[:], in_=tie_h),
            lambda g: g.dma_start(
                out=ctrl_sb[:, :].rearrange("p (c n) -> p c n", c=8),
                in_=ctrl_h.rearrange("(c p) n -> p c n", p=128)),
            lambda g: g.dma_start(out=biasT_sb[:], in_=biasT_h),
            lambda g: g.dma_start(out=gmask_sb[:], in_=gmask_h),
        ], dma=True)
        g0 = em.cnt["g"]

        E("v", [lambda v: v.memset(ones1[:], 1.0),
                lambda v: v.memset(onescol[:], 1.0),
                lambda v: v.memset(cst[:, 0:1], 1e-5),
                lambda v: v.memset(cst[:, 1:2], 1e-24),
                lambda v: v.memset(cst[:, 3:4], 10.0)])
        E("v", [lambda v: v.reciprocal(cst[:, 2:3], cst[:, 3:4])], selfdep=True)
        E("s", [
            lambda s: s.activation(abrow[:, 32:48], abrow[:, 0:16], AF.Sigmoid),
            lambda s: s.activation(abrow[:, 48:64], abrow[:, 16:32], AF.Sigmoid),
        ], waits={"g": g0})
        E("v", [
            lambda v: v.tensor_scalar_mul(abrow[:, 64:80], abrow[:, 32:48], 0.5),
            lambda v: v.tensor_scalar_mul(abrow[:, 80:96], abrow[:, 48:64], 2.0),
        ], waits={"s": em.cnt["s"]})
        E("v", [
            lambda v: v.tensor_scalar_mul(abrow[:, 96:112], abrow[:, 64:80], -1.0),
            lambda v: v.reciprocal(abrow[:, 48:64], abrow[:, 80:96]),
        ], selfdep=True)
        E("v", [
            lambda v: v.tensor_tensor(abrow[:, 112:128], abrow[:, 96:112], abrow[:, 48:64], ALU.mult),
        ], selfdep=True)
        vrows = em.cnt["v"]
        E("t", [
            lambda t: t.matmul(psIN[:, 0:16], ones1[:], abrow[:, 80:96], start=True, stop=True),
            lambda t: t.matmul(psIN[:, 16:32], ones1[:], abrow[:, 112:128], start=True, stop=True),
            lambda t: t.matmul(psIN[:, 32:48], ones1[:], abrow[:, 96:112], start=True, stop=True),
        ], waits={"v": vrows})
        E("v", [lambda v: v.tensor_copy(abbc[:, 0:48], psIN[:, 0:48])], waits={"t": em.cnt["t"]})
        vbc = em.cnt["v"]
        mask_ops = []
        for h in range(NH):
            hr = (h % 2) * 8 + h // 2
            mask_ops.append(lambda v, h=h, hr=hr: v.tensor_scalar_mul(
                mA0[:, hr * 64:(hr + 1) * 64], mlo128[:], abbc[:, 32 + h:33 + h]))
            mask_ops.append(lambda v, h=h, hr=hr: v.tensor_scalar_mul(
                mB0[:, hr * 64:(hr + 1) * 64], mup128[:], abbc[:, 32 + h:33 + h]))
        E("v", mask_ops, waits={"g": g0}, selfdep=True)
        init_v = em.cnt["v"]
        init_ctx.close()

        with ExitStack() as pa:
            wq_sb = pa.enter_context(nc.sbuf_tensor("wq_sb", [128, 8192], BF16))
            wk_sb = pa.enter_context(nc.sbuf_tensor("wk_sb", [128, 8192], BF16))
            wv_sb = pa.enter_context(nc.sbuf_tensor("wv_sb", [128, 8192], BF16))
            wo_sb = pa.enter_context(nc.sbuf_tensor("wo_sb", [128, 8192], BF16))
            H_sb = pa.enter_context(nc.sbuf_tensor("H_sb", [128, 1024], BF16))
            M_sb = pa.enter_context(nc.sbuf_tensor("M_sb", [128, 1024], BF16))
            x_sb = pa.enter_context(nc.sbuf_tensor("x_sb", [128, 4096], F32))
            xr_sb = x_sb
            xn_tm = pa.enter_context(nc.sbuf_tensor("xn_tm", [128, 4096], BF16))
            xn_fm = pa.enter_context(nc.sbuf_tensor("xn_fm", [128, 4096], BF16))
            q_fm = pa.enter_context(nc.sbuf_tensor("q_fm", [128, 4096], BF16))
            k_fm = pa.enter_context(nc.sbuf_tensor("k_fm", [128, 4096], BF16))
            k_tm = pa.enter_context(nc.sbuf_tensor("k_tm", [128, 4096], BF16))
            kb_tm = pa.enter_context(nc.sbuf_tensor("kb_tm", [128, 4096], BF16))
            v_tm = pa.enter_context(nc.sbuf_tensor("v_tm", [128, 4096], BF16))
            o_tm = xn_tm
            o_fm = xn_fm
            x1_sb = pa.enter_context(nc.sbuf_tensor("x1_sb", [128, 4096], BF16))
            sq_sb = pa.enter_context(nc.sbuf_tensor("sq_sb", [128, 1024], BF16))
            st_sb = pa.enter_context(nc.sbuf_tensor("st_sb", [128, 64], F32))
            rk_sb = pa.enter_context(nc.sbuf_tensor("rk_sb", [128, 64], F32))
            A_bufs = [pa.enter_context(nc.sbuf_tensor(f"A{k}", [128, 1024], BF16)) for k in range(6)]
            B_bufs = [pa.enter_context(nc.sbuf_tensor(f"B{k}", [128, 1024], BF16)) for k in range(5)]
            W_fin = [pa.enter_context(nc.sbuf_tensor(f"Wf{b}", [128, 1024], BF16)) for b in range(2)]
            PT_sb = pa.enter_context(nc.sbuf_tensor("PT_sb", [128, 1024], BF16))
            RHS_sb = pa.enter_context(nc.sbuf_tensor("RHS_sb", [128, 1024], BF16))
            tmp_f = pa.enter_context(nc.sbuf_tensor("tmp_f", [128, 1024], BF16))
            re_sb = pa.enter_context(nc.sbuf_tensor("re_sb", [128, 16], F32))

            psA = pa.enter_context(nc.psum_tensor("psA", [128, 1024], F32))
            psB = pa.enter_context(nc.psum_tensor("psB", [128, 1024], F32))
            psC = pa.enter_context(nc.psum_tensor("psC", [128, 1024], F32))
            psD = pa.enter_context(nc.psum_tensor("psD", [128, 512], F32))
            psT = pa.enter_context(nc.psum_tensor("psT", [128, 1024], BF16))

            E("g", [
                lambda g: g.dma_start(out=wq_sb[:, :].rearrange("p (c n) -> p c n", c=8),
                                      in_=wq_h.rearrange("(c p) n -> p c n", p=128)),
                lambda g: g.dma_start(out=wk_sb[:, :].rearrange("p (c n) -> p c n", c=8),
                                      in_=wk_h.rearrange("(c p) n -> p c n", p=128)),
                lambda g: g.dma_start(out=wv_sb[:, :].rearrange("p (c n) -> p c n", c=8),
                                      in_=wv_h.rearrange("(c p) n -> p c n", p=128)),
                lambda g: g.dma_start(out=wo_sb[:, :].rearrange("p (c n) -> p c n", c=8),
                                      in_=wo_h.rearrange("(c p) n -> p c n", p=128)),
            ], dma=True)
            gW = em.cnt["g"]
            E("v", [lambda v: v.memset(H_sb[:], 0.0), lambda v: v.memset(M_sb[:], 0.0)])

            for band in range(NBAND):
                r0 = band * BTB
                r1 = 2048 + band * BTB
                yx = E("y", [
                    lambda y: y.dma_start(
                        out=x_sb[:, 0:2048].rearrange("p (n m) -> p n m", n=2),
                        in_=x_h[r0:r0 + 256, :].rearrange("(n p) m -> p n m", p=128)),
                    lambda y: y.dma_start(
                        out=x_sb[:, 2048:4096].rearrange("p (n m) -> p n m", n=2),
                        in_=x_h[r1:r1 + 256, :].rearrange("(n p) m -> p n m", p=128)),
                ], dma=True, waits={"v": em.cnt["v"], "t": em.cnt["t"], "s": em.cnt["s"]})

                for tl in range(TPB):
                    xt = x_sb[:, tl * 1024:(tl + 1) * 1024]
                    sS = E("s", [lambda s, xt=xt: s.activation(sq_sb[:], xt, AF.Square)],
                           waits={"y": yx, "v": em.cnt["v"]})
                    E("v", [
                        lambda v, xt=xt, tl=tl: v.reduce_sum(st_sb[:, tl * 8:tl * 8 + 1].unsqueeze(2),
                                                             xt.unsqueeze(1), axis=AX),
                    ], waits={"y": yx})
                    E("v", [
                        lambda v, tl=tl: v.reduce_sum(st_sb[:, tl * 8 + 1:tl * 8 + 2].unsqueeze(2),
                                                      sq_sb[:, :].unsqueeze(1), axis=AX),
                    ], waits={"s": sS})
                vred = em.cnt["v"]
                stm = st_sb[:, 32:36]; ste = st_sb[:, 36:40]; stv = st_sb[:, 40:44]
                str_ = st_sb[:, 44:48]; stmr = st_sb[:, 48:52]
                s1 = st_sb[:, 0:32].rearrange("p (t k) -> p t k", k=8)[:, :, 0]
                s2 = st_sb[:, 0:32].rearrange("p (t k) -> p t k", k=8)[:, :, 1]
                E("v", [
                    lambda v: v.tensor_scalar_mul(stm, s1, 1.0 / D),
                    lambda v: v.tensor_scalar_mul(ste, s2, 1.0 / D),
                ], selfdep=True)
                E("v", [lambda v: v.tensor_tensor(stv, stm, stm, ALU.mult)], selfdep=True)
                E("v", [lambda v: v.tensor_sub(stv, ste, stv)], selfdep=True)
                E("s", [lambda s: s.activation(str_, stv, AF.Ln, bias=cst[:, 0:1])],
                  waits={"v": em.cnt["v"]})
                sLN = E("s", [lambda s: s.activation(str_, str_, AF.Exp, scale=-0.5)],
                        selfdep=True)
                E("v", [lambda v: v.tensor_tensor(stmr, stm, str_, ALU.mult)],
                  waits={"s": sLN}, selfdep=True)
                for tl in range(TPB):
                    E("v", [lambda v, tl=tl: v.tensor_scalar(
                        xn_tm[:, tl * 1024:(tl + 1) * 1024], x_sb[:, tl * 1024:(tl + 1) * 1024],
                        str_[:, tl:tl + 1], stmr[:, tl:tl + 1], ALU.mult, ALU.subtract)],
                      waits={"t": em.cnt["t"]}, selfdep=(tl == 0))
                vxn = em.cnt["v"]

                tT0 = em.cnt["t"]
                for kc in range(8):
                    tt = E("t", [
                        lambda t, kc=kc, tl2=tl2: t.transpose(
                            psT[:, tl2 * 128:(tl2 + 1) * 128],
                            xn_tm[:, tl2 * 1024 + kc * 128:tl2 * 1024 + kc * 128 + 128],
                            ident_bf[:])
                        for tl2 in range(TPB)
                    ], waits={"v": vxn} if kc == 0 else {"v": em.cnt["v"]})
                    E("v", [lambda v, kc=kc: v.tensor_copy(
                        xn_fm[:, kc * 512:(kc + 1) * 512], psT[:, 0:512])],
                      waits={"t": tt})
                vxnf = em.cnt["v"]

                for oc in range(8):
                    tq = E("t", [
                        lambda t, oc=oc, kc2=kc2: t.matmul(
                            psA[:, 0:512],
                            wq_sb[:, kc2 * 1024 + oc * 128:kc2 * 1024 + oc * 128 + 128],
                            xn_fm[:, kc2 * 512:(kc2 + 1) * 512],
                            start=(kc2 == 0), stop=(kc2 == 7))
                        for kc2 in range(8)
                    ], waits=({"v": vxnf, "g": gW} if oc == 0 else {"v": em.cnt["v"]}))
                    E("v", [lambda v, oc=oc: v.tensor_copy(
                        q_fm[:, oc * 512:(oc + 1) * 512], psA[:, 0:512])], waits={"t": tq})
                vq = em.cnt["v"]

                for tl in range(TPB):
                    tkv = E("t", [
                        (lambda t, tl=tl, kc2=kc2, which=w, half=hf: t.matmul(
                            (psB if which == 0 else psC)[:, half * 512:half * 512 + 512],
                            xn_fm[:, kc2 * 512 + tl * 128:kc2 * 512 + tl * 128 + 128],
                            (wk_sb if which == 0 else wv_sb)[:, kc2 * 1024 + half * 512:kc2 * 1024 + half * 512 + 512],
                            start=(kc2 == 0), stop=(kc2 == 7)))
                        for kc2 in range(8) for w in (0, 1) for hf in (0, 1)
                    ], waits={"v": em.cnt["v"], "g": gW, "s": em.cnt["s"]})
                    E("s", [lambda s: s.activation(sq_sb[:], psB[:, 0:1024], AF.Square)],
                      waits={"t": tkv})
                    E("v", [lambda v, tl=tl: v.reduce_sum(
                        rk_sb[:, 32:48].unsqueeze(2),
                        sq_sb[:, :].rearrange("p (a b) -> p a b", b=64), axis=AX)],
                      waits={"s": em.cnt["s"]})
                    E("v", [lambda v: v.tensor_scalar_max(rk_sb[:, 32:48], rk_sb[:, 32:48], cst[:, 1:2])],
                      selfdep=True)
                    E("s", [lambda s: s.activation(rk_sb[:, 0:16], rk_sb[:, 32:48], AF.Ln)],
                      waits={"v": em.cnt["v"]})
                    sRK = E("s", [lambda s: s.activation(rk_sb[:, 0:16], rk_sb[:, 0:16], AF.Exp, scale=-0.5)],
                            selfdep=True)
                    E("v", [lambda v: v.tensor_tensor(rk_sb[:, 16:32], rk_sb[:, 0:16], abbc[:, 0:16], ALU.mult)],
                      waits={"s": sRK})
                    _rkb = em.cnt["v"]
                    E("v", [
                        lambda v, tl=tl: v.tensor_tensor(
                            k_tm[:, tl * 1024:(tl + 1) * 1024].rearrange("p (a b) -> p a b", b=64),
                            psB[:, 0:1024].rearrange("p (a b) -> p a b", b=64),
                            rk_sb[:, 0:16].unsqueeze(2).broadcast_to([128, 16, 64]),
                            ALU.mult),
                        lambda v, tl=tl: v.tensor_tensor(
                            kb_tm[:, tl * 1024:(tl + 1) * 1024].rearrange("p (a b) -> p a b", b=64),
                            psB[:, 0:1024].rearrange("p (a b) -> p a b", b=64),
                            rk_sb[:, 16:32].unsqueeze(2).broadcast_to([128, 16, 64]),
                            ALU.mult),
                    ], selfdep=True)
                    E("s", [lambda s, tl=tl: s.activation(
                        v_tm[:, tl * 1024:(tl + 1) * 1024], psC[:, 0:1024], AF.Copy)])
                vkv = em.cnt["v"]
                skv = em.cnt["s"]

                for oc in range(8):
                    tt = E("t", [
                        lambda t, oc=oc, tl2=tl2: t.transpose(
                            psT[:, tl2 * 128:(tl2 + 1) * 128],
                            k_tm[:, tl2 * 1024 + oc * 128:tl2 * 1024 + oc * 128 + 128],
                            ident_bf[:])
                        for tl2 in range(TPB)
                    ], waits={"v": vkv if oc == 0 else em.cnt["v"], "s": skv})
                    E("v", [lambda v, oc=oc: v.tensor_copy(
                        k_fm[:, oc * 512:(oc + 1) * 512], psT[:, 0:512])], waits={"t": tt})
                vkf = em.cnt["v"]

                if dbg_upto == "qkv" and band == 0:
                    _emit_dbg_qkv(nc, em, E, locals(), dbg)
                    break

                for cp in range(2):
                    for b in range(2):
                        gfns = []
                        for h in range(NH):
                            hb = (h % 2) * 64
                            hr = (h % 2) * 8 + h // 2
                            col = (h // 2) * 512 + b * 256
                            for ci in range(2):
                                c = 2 * cp + ci
                                ksl = k_fm[hb:hb + 64, col + c * 64:col + c * 64 + 64]
                                gfns.append(lambda t, ksl=ksl, hr=hr, h=h, ci=ci: t.matmul(
                                    psA[ci * 64:ci * 64 + 64, hr * 64:hr * 64 + 64],
                                    ksl, ksl, start=True, stop=True,
                                    tile_position=((h % 2) * 64, ci * 64)))
                        tG = E("t", gfns, waits={"v": em.cnt["v"], "s": em.cnt["s"]})
                        vM = E("v", [
                            lambda v: v.tensor_tensor(A_bufs[0][:], psA[:, 0:1024], mA0[:], ALU.mult),
                            lambda v: v.tensor_tensor(B_bufs[0][:], psA[:, 0:1024], mB0[:], ALU.mult),
                        ], waits={"t": tG, "v": init_v})
                        if dbg_upto == "gmask":
                            break
                        for k in range(5):
                            sq_fns = []
                            for h in range(NH):
                                for par in (0, 64):
                                    sq_fns.append(lambda t, k=k, h=h, par=par: t.matmul(
                                        psB[par:par + 64, h * 64:h * 64 + 64],
                                        B_bufs[k][par:par + 64, h * 64:h * 64 + 64],
                                        A_bufs[k][par:par + 64, h * 64:h * 64 + 64],
                                        start=True, stop=True, tile_position=(par, par)))
                                    if k < 4:
                                        sq_fns.append(lambda t, k=k, h=h, par=par: t.matmul(
                                            psC[par:par + 64, h * 64:h * 64 + 64],
                                            A_bufs[k][par:par + 64, h * 64:h * 64 + 64],
                                            B_bufs[k][par:par + 64, h * 64:h * 64 + 64],
                                            start=True, stop=True, tile_position=(par, par)))
                            tS = E("t", sq_fns, waits={"v": vM, "s": em.cnt["s"]})
                            if k < 4:
                                E("s", [lambda s, k=k: s.copy(B_bufs[k + 1][:], psC[:, 0:1024])],
                                  waits={"t": tS})
                            vM = E("v", [lambda v, k=k: v.tensor_copy(A_bufs[k + 1][:], psB[:, 0:1024])],
                                   waits={"t": tS})
                        kb_cols = [None, None]
                        for ci in range(2):
                            c = 2 * cp + ci
                            tl = b * 2 + c // 2
                            kb_cols[ci] = tl * 1024
                        tlp = b * 2 + cp
                        kbt = kb_tm[:, tlp * 1024:(tlp + 1) * 1024]
                        Wcur = None
                        for k in range(6):
                            ap_fns = []
                            for h in range(NH):
                                hr = (h % 2) * 8 + h // 2
                                rsl = (kbt[:, h * 64:h * 64 + 64] if k == 0
                                       else Wcur[:, hr * 64:hr * 64 + 64])
                                for par in (0, 64):
                                    ap_fns.append(lambda t, k=k, hr=hr, par=par, rsl=rsl: t.matmul(
                                        psB[par:par + 64, hr * 64:hr * 64 + 64],
                                        A_bufs[k][par:par + 64, hr * 64:hr * 64 + 64],
                                        rsl[par:par + 64, :],
                                        start=True, stop=True, tile_position=(par, par)))
                            tA = E("t", ap_fns, waits={"v": vM, "s": em.cnt["s"]})
                            Wnext = W_fin[b] if k == 5 else A_bufs[k]
                            if k == 0:
                                vM = E("v", [lambda v, Wnext=Wnext, kbt=kbt: v.tensor_tensor(
                                    Wnext[:, :].rearrange("p (b a c) -> p b a c", b=2, a=8),
                                    psB[:, 0:1024].rearrange("p (b a c) -> p b a c", b=2, a=8),
                                    kbt.rearrange("p (a b c) -> p b a c", a=8, b=2),
                                    ALU.add)], waits={"t": tA}, selfdep=True)
                            else:
                                vM = E("v", [lambda v, Wnext=Wnext, Wcur=Wcur: v.tensor_tensor(
                                    Wnext[:], psB[:, 0:1024], Wcur[:], ALU.add)],
                                    waits={"t": tA}, selfdep=True)
                            Wcur = Wnext
                        if dbg_upto == "solve" and cp == 0 and b == 0:
                            break
                    vSolve = em.cnt["v"]
                    if dbg_upto in ("solve", "gmask") and cp == 0:
                        break

                    for ci in range(2):
                        c = 2 * cp + ci
                        cb = ci * 64
                        pbase = (c % 2) * 64
                        for b in range(2):
                            tl = b * 2 + c // 2
                            fns = []
                            for h in range(NH):
                                hb = (h % 2) * 64
                                hr = (h % 2) * 8 + h // 2
                                fcol = (h // 2) * 512 + b * 256 + c * 64
                                ksl = k_fm[hb:hb + 64, fcol:fcol + 64]
                                qsl = q_fm[hb:hb + 64, fcol:fcol + 64]
                                fns.append(lambda t, ksl=ksl, qsl=qsl, hr=hr, h=h, pbase=pbase: t.matmul(
                                    psA[pbase:pbase + 64, hr * 64:hr * 64 + 64], ksl, qsl,
                                    start=True, stop=True, tile_position=((h % 2) * 64, pbase)))
                                fns.append(lambda t, ksl=ksl, hr=hr, h=h, b=b, pbase=pbase: t.matmul(
                                    psB[pbase:pbase + 64, hr * 64:hr * 64 + 64], ksl,
                                    H_sb[(h % 2) * 64:(h % 2) * 64 + 64, b * 512 + (h // 2) * 64:b * 512 + (h // 2) * 64 + 64],
                                    start=True, stop=True, tile_position=((h % 2) * 64, pbase)))
                                fns.append(lambda t, qsl=qsl, hr=hr, h=h, b=b, pbase=pbase: t.matmul(
                                    psC[pbase:pbase + 64, hr * 64:hr * 64 + 64], qsl,
                                    M_sb[(h % 2) * 64:(h % 2) * 64 + 64, b * 512 + (h // 2) * 64:b * 512 + (h // 2) * 64 + 64],
                                    start=True, stop=True, tile_position=((h % 2) * 64, pbase)))
                            tSMm = E("t", fns, waits={"v": em.cnt["v"]})
                            sE = E("s", [lambda s, pbase=pbase: s.activation(
                                PT_sb[pbase:pbase + 64, :], psA[pbase:pbase + 64, 0:1024],
                                AF.Exp, scale=0.125)], waits={"t": tSMm})
                            vP = E("v", [
                                lambda v, pbase=pbase: v.tensor_tensor(
                                    PT_sb[pbase:pbase + 64, :].rearrange("p (a b) -> p a b", b=64),
                                    PT_sb[pbase:pbase + 64, :].rearrange("p (a b) -> p a b", b=64),
                                    maskj_bf[pbase:pbase + 64, :].unsqueeze(1).broadcast_to([64, 16, 64]),
                                    ALU.mult),
                                lambda v, pbase=pbase: v.tensor_tensor(
                                    tmp_f[pbase:pbase + 64, :].rearrange("p (b a c) -> p b a c", b=2, a=8),
                                    psB[pbase:pbase + 64, 0:1024].rearrange("p (b a c) -> p b a c", b=2, a=8),
                                    abbc[pbase:pbase + 64, 16:32].rearrange("p (a b) -> p b a", a=8)
                                        .unsqueeze(3).broadcast_to([64, 2, 8, 64]),
                                    ALU.mult),
                            ], waits={"s": sE, "t": tSMm})
                            vR = E("v", [
                                lambda v, tl=tl, pbase=pbase: v.tensor_add(
                                    RHS_sb[pbase:pbase + 64, :].rearrange("p (b a c) -> p b a c", b=2, a=8),
                                    tmp_f[pbase:pbase + 64, :].rearrange("p (b a c) -> p b a c", b=2, a=8),
                                    v_tm[pbase:pbase + 64, tl * 1024:(tl + 1) * 1024]
                                        .rearrange("p (a b c) -> p b a c", a=8, b=2)),
                            ], selfdep=True)
                            fns = []
                            for h in range(NH):
                                hr = (h % 2) * 8 + h // 2
                                psl = PT_sb[pbase:pbase + 64, hr * 64:hr * 64 + 64]
                                fns.append(lambda t, psl=psl, hr=hr, pbase=pbase: t.matmul(
                                    psD[pbase:pbase + 64, hr:hr + 1], psl, onescol[pbase:pbase + 64, :],
                                    start=True, stop=True, tile_position=(pbase, pbase)))
                                fns.append(lambda t, psl=psl, hr=hr, h=h, tl=tl, pbase=pbase: t.matmul(
                                    psA[pbase:pbase + 64, hr * 64:hr * 64 + 64], psl,
                                    v_tm[pbase:pbase + 64, tl * 1024 + h * 64:tl * 1024 + h * 64 + 64],
                                    start=True, stop=True, tile_position=(pbase, pbase)))
                            tPV = E("t", fns, waits={"v": vP, "s": sE})
                            fns = []
                            for h in range(NH):
                                hr = (h % 2) * 8 + h // 2
                                wsl = W_fin[b][pbase:pbase + 64, hr * 64:hr * 64 + 64]
                                rsl = RHS_sb[pbase:pbase + 64, hr * 64:hr * 64 + 64]
                                fns.append(lambda t, wsl=wsl, rsl=rsl, h=h, pbase=pbase: t.matmul(
                                    psB[(h % 2) * 64:(h % 2) * 64 + 64, (h // 2) * 64:(h // 2) * 64 + 64],
                                    wsl, rsl, start=True, stop=True, tile_position=(pbase, (h % 2) * 64)))
                                fns.append(lambda t, wsl=wsl, rsl=rsl, h=h, pbase=pbase: t.matmul(
                                    psB[(h % 2) * 64:(h % 2) * 64 + 64, 512 + (h // 2) * 64:512 + (h // 2) * 64 + 64],
                                    rsl, wsl, start=True, stop=True, tile_position=(pbase, (h % 2) * 64)))
                            tHM = E("t", fns, waits={"v": vR})
                            vRe = E("v", [lambda v, pbase=pbase: v.reciprocal(
                                re_sb[pbase:pbase + 64, 0:16], psD[pbase:pbase + 64, 0:16])],
                                waits={"t": tPV})
                            E("v", [
                                lambda v, pbase=pbase: v.tensor_tensor(
                                    tmp_f[pbase:pbase + 64, :].rearrange("p (a b) -> p a b", b=64),
                                    psA[pbase:pbase + 64, 0:1024].rearrange("p (a b) -> p a b", b=64),
                                    re_sb[pbase:pbase + 64, 0:16].unsqueeze(2).broadcast_to([64, 16, 64]),
                                    ALU.mult),
                            ], selfdep=True)
                            E("v", [
                                lambda v, tl=tl, pbase=pbase: v.tensor_add(
                                    o_tm[pbase:pbase + 64, tl * 1024:(tl + 1) * 1024]
                                        .rearrange("p (a b c) -> p b a c", a=8, b=2),
                                    tmp_f[pbase:pbase + 64, :]
                                        .rearrange("p (b a c) -> p b a c", b=2, a=8),
                                    psC[pbase:pbase + 64, 0:1024]
                                        .rearrange("p (b a c) -> p b a c", b=2, a=8)),
                            ], selfdep=True)
                            E("v", [
                                lambda v, b=b: v.tensor_add(
                                    H_sb[:, b * 512:(b + 1) * 512],
                                    H_sb[:, b * 512:(b + 1) * 512],
                                    psB[:, 0:512]),
                                lambda v, b=b: v.scalar_tensor_tensor(
                                    M_sb[:, b * 512:(b + 1) * 512],
                                    psB[:, 512:1024], cst[:, 2:3],
                                    M_sb[:, b * 512:(b + 1) * 512],
                                    ALU.mult, ALU.add),
                            ], waits={"t": tHM})

                if dbg_upto in ("solve", "gmask") and band == 0:
                    _emit_dbg_solve(nc, em, E, locals(), dbg)
                    break
                if dbg_upto == "chain" and band == 0:
                    _emit_dbg_chain(nc, em, E, locals(), dbg)
                    break

                vO = em.cnt["v"]
                for oc in range(8):
                    tt = E("t", [
                        lambda t, oc=oc, tl2=tl2: t.transpose(
                            psT[:, tl2 * 128:(tl2 + 1) * 128],
                            o_tm[:, tl2 * 1024 + oc * 128:tl2 * 1024 + oc * 128 + 128],
                            ident_bf[:])
                        for tl2 in range(TPB)
                    ], waits={"v": vO if oc == 0 else em.cnt["v"]})
                    E("v", [lambda v, oc=oc: v.tensor_copy(
                        o_fm[:, oc * 512:(oc + 1) * 512], psT[:, 0:512])], waits={"t": tt})
                vOf = em.cnt["v"]
                yxr = em.cnt["y"]
                for tl in range(TPB):
                    tW = E("t", [
                        (lambda t, tl=tl, oc2=oc2, half=hf: t.matmul(
                            psA[:, half * 512:half * 512 + 512],
                            o_fm[:, oc2 * 512 + tl * 128:oc2 * 512 + tl * 128 + 128],
                            wo_sb[:, oc2 * 1024 + half * 512:oc2 * 1024 + half * 512 + 512],
                            start=(oc2 == 0), stop=(oc2 == 7)))
                        for oc2 in range(8) for hf in (0, 1)
                    ], waits={"v": em.cnt["v"]})
                    E("v", [lambda v, tl=tl: v.tensor_add(
                        x1_sb[:, tl * 1024:(tl + 1) * 1024],
                        xr_sb[:, tl * 1024:(tl + 1) * 1024],
                        psA[:, 0:1024])], waits={"t": tW})
                vx1 = em.cnt["v"]
                E("y", [
                    lambda y: y.dma_start(
                        out=x1_d[r0:r0 + 256, :].rearrange("(n p) m -> p n m", p=128),
                        in_=x1_sb[:, 0:2048].rearrange("p (n m) -> p n m", n=2)),
                    lambda y: y.dma_start(
                        out=x1_d[r1:r1 + 256, :].rearrange("(n p) m -> p n m", p=128),
                        in_=x1_sb[:, 2048:4096].rearrange("p (n m) -> p n m", n=2)),
                ], dma=True, waits={"v": vx1})
                if dbg_upto == "band1":
                    break

            em.barrier()

        if dbg_upto in ("qkv", "chain", "solve", "gmask"):
            _finalize_wait(em, E)
            return nc, dbg_outs
        if dbg_upto in ("phasea", "band1"):
            a = nc.dram_tensor("dbg_x1", [TC, D], BF16, kind="ExternalOutput").ap()
            dbg_outs.append("dbg_x1")
            E("y", [lambda y: y.dma_start(out=a, in_=x1_d)], dma=True, waits=em.snap())
            _finalize_wait(em, E)
            return nc, dbg_outs

        with ExitStack() as pb:
            sel_sb = pb.enter_context(nc.sbuf_tensor("sel_sb", [128, 65536], BF16))
            pselT_sb = pb.enter_context(nc.sbuf_tensor("pselT_sb", [128, 8192], BF16))
            psel_sb = pb.enter_context(nc.sbuf_tensor("psel_sb", [128, 512], BF16))
            x1t_sb = pb.enter_context(nc.sbuf_tensor("x1t_sb", [128, 1024], BF16))
            y_sb = pb.enter_context(nc.sbuf_tensor("y_sb", [128, 1024], BF16))
            yfbuf = pb.enter_context(nc.sbuf_tensor("yfbuf", [128, 2048], BF16))
            yf_sb = yfbuf[:, 0:1024]
            sqb_sb = yf_sb
            stb_sb = pb.enter_context(nc.sbuf_tensor("stb_sb", [128, 16], F32))
            perm_sb = pb.enter_context(nc.sbuf_tensor("perm_sb", [64, 128], BF16))
            permT_sb = pb.enter_context(nc.sbuf_tensor("permT_sb", [128, 64], BF16))
            mx_sb = pb.enter_context(nc.sbuf_tensor("mx_sb", [64, 8], F32))
            f1a = pb.enter_context(nc.sbuf_tensor("f1a", [128, 4096], BF16))
            f1_sb = [f1a[:, 0:2048], pb.enter_context(nc.sbuf_tensor("f1_1", [128, 2048], BF16))[:, :]]
            f2_sb = [pb.enter_context(nc.sbuf_tensor(f"f2_{i}", [64, 4096], BF16)) for i in range(2)]
            inner_sb = pb.enter_context(nc.sbuf_tensor("inner_sb", [64, 128], BF16))
            scat_sb = f1a
            inter_sb = pb.enter_context(nc.sbuf_tensor("inter_sb", [128, 4096], BF16))
            outf_sb = yfbuf[:, :].bitcast(F32)

            pb0 = pb.enter_context(nc.psum_tensor("pb0", [128, 512], F32))
            pb1 = pb.enter_context(nc.psum_tensor("pb1", [128, 512], F32))
            pb2 = pb.enter_context(nc.psum_tensor("pb2", [128, 512], F32))
            pb3 = pb.enter_context(nc.psum_tensor("pb3", [128, 1024], F32))
            pb4 = pb.enter_context(nc.psum_tensor("pb4", [128, 1024], F32))
            pbT = pb.enter_context(nc.psum_tensor("pbT", [128, 1024], BF16))

            for half in range(2):
                for tl in range(16):
                    gt = half * 16 + tl
                    row = gt * 128
                    yx1 = E("y", [lambda y, row=row: y.dma_start(
                        out=x1t_sb[:], in_=x1_d[row:row + 128, :])],
                        dma=True, waits={"t": em.cnt["t"], "v": em.cnt["v"], "s": em.cnt["s"]})
                    sS = E("s", [lambda s: s.activation(sqb_sb, x1t_sb[:], AF.Square)],
                           waits={"y": yx1})
                    E("v", [lambda v: v.reduce_sum(stb_sb[:, 0:1].unsqueeze(2),
                                                   x1t_sb[:, :].unsqueeze(1), axis=AX)],
                      waits={"y": yx1})
                    E("v", [lambda v: v.reduce_sum(stb_sb[:, 1:2].unsqueeze(2),
                                                   sqb_sb.unsqueeze(1), axis=AX)],
                      waits={"s": sS})
                    E("v", [
                        lambda v: v.tensor_scalar_mul(stb_sb[:, 2:3], stb_sb[:, 0:1], 1.0 / D),
                        lambda v: v.tensor_scalar_mul(stb_sb[:, 3:4], stb_sb[:, 1:2], 1.0 / D),
                    ], selfdep=True)
                    E("v", [lambda v: v.tensor_tensor(stb_sb[:, 4:5], stb_sb[:, 2:3], stb_sb[:, 2:3], ALU.mult)], selfdep=True)
                    E("v", [lambda v: v.tensor_sub(stb_sb[:, 4:5], stb_sb[:, 3:4], stb_sb[:, 4:5])], selfdep=True)
                    E("s", [lambda s: s.activation(stb_sb[:, 5:6], stb_sb[:, 4:5], AF.Ln, bias=cst[:, 0:1])],
                      waits={"v": em.cnt["v"]})
                    sLN = E("s", [lambda s: s.activation(stb_sb[:, 5:6], stb_sb[:, 5:6], AF.Exp, scale=-0.5)],
                            selfdep=True)
                    E("v", [lambda v: v.tensor_tensor(stb_sb[:, 6:7], stb_sb[:, 2:3], stb_sb[:, 5:6], ALU.mult)],
                      waits={"s": sLN}, selfdep=True)
                    vy = E("v", [lambda v: v.tensor_scalar(
                        y_sb[:], x1t_sb[:], stb_sb[:, 5:6], stb_sb[:, 6:7], ALU.mult, ALU.subtract)],
                        selfdep=True)
                    tT = E("t", [
                        lambda t, kc=kc: t.transpose(
                            pbT[:, kc * 128:(kc + 1) * 128],
                            y_sb[:, kc * 128:(kc + 1) * 128], ident_bf[:])
                        for kc in range(8)
                    ], waits={"v": vy})
                    vyf = E("v", [lambda v: v.tensor_copy(yf_sb, pbT[:, 0:1024])], waits={"t": tT})
                    tL = E("t", [
                        lambda t, kc=kc: t.matmul(
                            pb0[0:64, 0:128],
                            ctrl_sb[:, kc * 64:(kc + 1) * 64],
                            yf_sb[:, kc * 128:(kc + 1) * 128],
                            start=(kc == 0), stop=False)
                        for kc in range(8)
                    ] + [lambda t: t.matmul(
                        pb0[0:64, 0:128], ones1[:, 0:64], tie_sb[:, 0:128],
                        start=False, stop=True)], waits={"v": vyf})
                    vmx = E("v", [lambda v: v.reduce_max(
                        mx_sb[:, :].unsqueeze(2),
                        pb0[0:64, 0:128].rearrange("p (a b) -> p a b", b=16), axis=AX)],
                        waits={"t": tL})
                    vpm = E("v", [lambda v: v.tensor_tensor(
                        perm_sb[:, :].rearrange("p (a b) -> p a b", b=16),
                        pb0[0:64, 0:128].rearrange("p (a b) -> p a b", b=16),
                        mx_sb[:, :].unsqueeze(2).broadcast_to([64, 8, 16]),
                        ALU.is_equal)], selfdep=True)
                    tPT = E("t", [lambda t: t.transpose(
                        pbT[:, 0:64], perm_sb[:], ident_bf[0:64, 0:64])], waits={"v": vpm})
                    E("v", [lambda v: v.tensor_copy(permT_sb[:], pbT[:, 0:64])],
                      waits={"t": tPT}, selfdep=True)
                    vPS = E("v", [lambda v: v.tensor_tensor(
                        psel_sb[:, :].rearrange("p (e g) -> p e g", g=8),
                        permT_sb[:, :].unsqueeze(2).broadcast_to([128, 64, 8]),
                        gmask_sb[:, :].unsqueeze(1).broadcast_to([128, 64, 8]),
                        ALU.mult)], selfdep=True)
                    tPs = E("t", [
                        lambda t, k=k: t.transpose(
                            pbT[:, k * 128:(k + 1) * 128],
                            psel_sb[:, k * 128:(k + 1) * 128], ident_bf[:])
                        for k in range(4)
                    ], waits={"v": vPS})
                    E("v", [lambda v, tl=tl: v.tensor_copy(
                        pselT_sb[:, tl * 512:(tl + 1) * 512], pbT[:, 0:512])], waits={"t": tPs})
                    for kc in range(8):
                        pg = pb1 if kc % 2 == 0 else pb2
                        tg = E("t", [lambda t, kc=kc, pg=pg: t.matmul(
                            pg[:, 0:512], y_sb[:, kc * 128:(kc + 1) * 128], psel_sb[:],
                            start=True, stop=True)],
                            waits={"v": em.cnt["v"], "s": em.cnt["s"]})
                        ek2 = "v" if kc % 2 == 0 else "s"
                        if kc % 2 == 0:
                            E("v", [lambda v, kc=kc, tl=tl, pg=pg: v.tensor_copy(
                                sel_sb[:, (kc * 16 + tl) * 512:(kc * 16 + tl) * 512 + 512],
                                pg[:, 0:512])], waits={"t": tg})
                        else:
                            E("s", [lambda s, kc=kc, tl=tl, pg=pg: s.copy(
                                sel_sb[:, (kc * 16 + tl) * 512:(kc * 16 + tl) * 512 + 512],
                                pg[:, 0:512])], waits={"t": tg})
                vGather = em.cnt["v"]

                def f1_load(g, j, buf):
                    return g.dma_start(
                        out=f1_sb[buf].rearrange("p (c F) -> p c F", c=8),
                        in_=f1_h[:, j * 256:(j + 1) * 256].rearrange("(c p) F -> p c F", p=128))

                def f2_load(g, j, buf):
                    return g.dma_start(
                        out=f2_sb[buf][:, :].rearrange("p (el m) -> p el m", el=4),
                        in_=f2_h[j * 256:(j + 1) * 256, :].rearrange("(el p) m -> p el m", p=64))

                gF = E("g", [lambda g: f1_load(g, 0, 0), lambda g: f2_load(g, 0, 0)],
                       dma=True, waits={"t": em.cnt["t"]})
                for j in range(16):
                    buf = j % 2
                    if j < 15:
                        gF = E("g", [lambda g, j=j, buf=buf: f1_load(g, j + 1, 1 - buf),
                                     lambda g, j=j, buf=buf: f2_load(g, j + 1, 1 - buf)],
                               dma=True, waits={"t": em.cnt["t"]})
                        gw = gF - 32
                    else:
                        gw = gF
                    for el in range(4):
                        es = j * 4 + el
                        tF1 = E("t", [
                            lambda t, kc=kc, el=el, buf=buf: t.matmul(
                                pb0[0:64, 128:256],
                                f1_sb[buf][:, kc * 256 + el * 64:kc * 256 + el * 64 + 64],
                                sel_sb[:, :].rearrange("p (c l g) -> p c l g", c=8, l=16)[:, kc, :, :]
                                      .rearrange("p l (e g) -> p l e g", g=8)[:, :, es, :],
                                start=(kc == 0), stop=(kc == 7))
                            for kc in range(8)
                        ], waits={"v": vGather, "g": gw, "s": em.cnt["s"]})
                        sR = E("s", [lambda s, es=es: s.activation(
                            inner_sb[:], pb0[0:64, 128:256], AF.Relu, bias=biasT_sb[:, es:es + 1])],
                            waits={"t": tF1})
                        tF2 = E("t", [
                            lambda t, hf=hf, el=el, buf=buf: t.matmul(
                                pb3[:, hf * 512:hf * 512 + 512], inner_sb[:],
                                f2_sb[buf][:, el * 1024 + hf * 512:el * 1024 + hf * 512 + 512],
                                start=True, stop=True)
                            for hf in (0, 1)
                        ], waits={"s": sR, "v": em.cnt["v"]})
                        E("v", [lambda v, el=el: v.tensor_copy(
                            inter_sb[:, el * 1024:(el + 1) * 1024], pb3[:, 0:1024])],
                          waits={"t": tF2, "y": em.cnt["y"]})
                    vI8 = em.cnt["v"]
                    e0 = (j % 4) * 4
                    kblk = j // 4
                    E("y", [lambda y, e0=e0, kblk=kblk, half=half: y.dma_start(
                        out=inter_d.rearrange("(e R k) m -> e R k m", e=16, R=256)[
                            e0:e0 + 4, half * 128:half * 128 + 128, kblk, :].transpose([1, 0, 2]),
                        in_=inter_sb[:, :].rearrange("p (el m) -> p el m", el=4))],
                      dma=True, waits={"v": vI8})
                yInter = em.cnt["y"]

                for tl in range(16):
                    gt = half * 16 + tl
                    row = gt * 128
                    ysc = E("y", [
                        lambda y, tl=tl, half=half: y.dma_start(
                            out=scat_sb[:, :],
                            in_=inter_d.rearrange("(e R k) m -> e R (k m)", e=16, R=256)[
                                :, half * 128 + tl * 8:half * 128 + tl * 8 + 8, :]),
                        lambda y, row=row: y.dma_start(out=x1t_sb[:], in_=x1_d[row:row + 128, :]),
                    ], dma=True, waits={"y": yInter, "v": em.cnt["v"], "t": em.cnt["t"]})
                    po = pb3 if tl % 2 == 0 else pb4
                    tSc = E("t", [
                        (lambda t, k=k, hf=hf, tl=tl, po=po: t.matmul(
                            po[:, hf * 512:hf * 512 + 512],
                            pselT_sb[:, tl * 512 + k * 128:tl * 512 + k * 128 + 128],
                            scat_sb[:, k * 1024 + hf * 512:k * 1024 + hf * 512 + 512],
                            start=(k == 0), stop=(k == 3)))
                        for k in range(4) for hf in (0, 1)
                    ], waits={"y": ysc})
                    vF = E("v", [lambda v, po=po: v.tensor_add(outf_sb, x1t_sb[:], po[:, 0:1024])],
                           waits={"t": tSc, "y": ysc})
                    E("y", [lambda y, row=row: y.dma_start(out=out_h[row:row + 128, :], in_=outf_sb)],
                      dma=True, waits={"v": vF})

        _finalize_wait(em, E)
    return nc, dbg_outs


def _finalize_wait(em, E):
    E("t", [lambda t: t.nop()], waits=em.snap(), noinc=True)


def _emit_dbg_qkv(nc, em, E, L, dbg):
    outs = {}
    for nm in ("q_fm", "k_tm", "kb_tm", "v_tm", "k_fm", "xn_tm", "xn_fm"):
        outs[nm] = dbg("dbg_" + nm, [128, 4096])
    w = em.snap()
    E("y", [
        (lambda y, a=outs[nm], t=L[nm]: y.dma_start(out=a, in_=t[:, :]))
        for nm in outs
    ], dma=True, waits=w)


def _emit_dbg_solve(nc, em, E, L, dbg):
    outs = {}
    outs["W_fin0"] = dbg("dbg_W_fin0", [128, 1024])
    outs["A0"] = dbg("dbg_A0", [128, 1024])
    outs["B0"] = dbg("dbg_B0", [128, 1024])
    w = em.snap()
    E("y", [lambda y, a=outs["W_fin0"], t=L["W_fin"][0]: y.dma_start(out=a, in_=t[:, :]),
            lambda y, a=outs["A0"], t=L["A_bufs"][0]: y.dma_start(out=a, in_=t[:, :]),
            lambda y, a=outs["B0"], t=L["A_bufs"][1]: y.dma_start(out=a, in_=t[:, :])],
      dma=True, waits=w)


def _emit_dbg_chain(nc, em, E, L, dbg):
    outs = {}
    for nm in ("o_tm", "q_fm", "k_fm", "v_tm"):
        outs[nm] = dbg("dbg_" + nm, [128, 4096])
    for nm in ("H_sb", "M_sb"):
        outs[nm] = dbg("dbg_" + nm, [128, 1024])
    w = em.snap()
    E("y", [
        (lambda y, a=outs[nm], t=L[nm]: y.dma_start(out=a, in_=t[:, :]))
        for nm in outs
    ], dma=True, waits=w)



def _consts():
    ident = np.eye(128, dtype=np.float32)
    maskj = np.triu(np.ones((64, 64), np.float32))
    masklo = np.tril(np.ones((64, 64), np.float32), -1)
    maskup = np.triu(np.ones((64, 64), np.float32), 1)
    tie = np.tile(np.linspace(0.0, 1e-6, 16, dtype=np.float32), 8)[None, :]
    gmask = (np.arange(128)[:, None] // 16 == np.arange(8)[None, :]).astype(np.float32)
    return ident, maskj, masklo, maskup, tie, gmask


def kernel(L0_attn_x, L0_attn_Wq, L0_attn_Wk, L0_attn_Wv, L0_attn_Wo,
           L0_attn_alpha_raw, L0_attn_beta_raw,
           L0_ffn_controller, L0_ffn_f1, L0_ffn_f2, L0_ffn_bias):
    x = np.ascontiguousarray(np.asarray(L0_attn_x, np.float32))
    wqT = np.ascontiguousarray(np.asarray(L0_attn_Wq, np.float32).T)
    wkT = np.ascontiguousarray(np.asarray(L0_attn_Wk, np.float32).T)
    wvT = np.ascontiguousarray(np.asarray(L0_attn_Wv, np.float32).T)
    woT = np.ascontiguousarray(np.asarray(L0_attn_Wo, np.float32).T)
    ctrl = np.ascontiguousarray(np.asarray(L0_ffn_controller, np.float32).reshape(D, 64))
    f1w = np.ascontiguousarray(np.asarray(L0_ffn_f1, np.float32).reshape(D, 4096))
    f2w = np.ascontiguousarray(np.asarray(L0_ffn_f2, np.float32).reshape(4096, D))
    biasT = np.ascontiguousarray(np.asarray(L0_ffn_bias, np.float32).reshape(64, 64).T)
    araw = np.ascontiguousarray(np.asarray(L0_attn_alpha_raw, np.float32).reshape(1, 16))
    braw = np.ascontiguousarray(np.asarray(L0_attn_beta_raw, np.float32).reshape(1, 16))
    ident, maskj, masklo, maskup, tie, gmask = _consts()

    nc, _ = build_nc()
    in_maps = []
    for c in range(NCORE):
        xc = np.ascontiguousarray(x[2 * c:2 * c + 2].reshape(TC, D))
        in_maps.append(dict(
            x=xc, wqT=wqT, wkT=wkT, wvT=wvT, woT=woT, ctrl=ctrl, f1w=f1w,
            f2w=f2w, biasT=biasT, araw=araw, braw=braw, ident=ident,
            maskj=maskj, masklo=masklo, maskup=maskup, tie=tie, gmask=gmask))
    res = run_bass_kernel_spmd(nc, in_maps, core_ids=list(range(NCORE)))
    out = np.empty((16, 2048, D), np.float32)
    for c in range(NCORE):
        out[2 * c:2 * c + 2] = np.asarray(res.results[c]["out"], np.float32).reshape(2, 2048, D)
    return out


# version 09
# speedup vs baseline: 1.0174x; 1.0174x over previous
import numpy as np
import ml_dtypes
from contextlib import ExitStack
import concourse.bass as bass
import concourse.mybir as mybir
from concourse.bass_utils import run_bass_kernel_spmd

F32 = mybir.dt.float32
BF16 = mybir.dt.bfloat16
AF = mybir.ActivationFunctionType
ALU = mybir.AluOpType
AX = mybir.AxisListType.X
bf16 = ml_dtypes.bfloat16

NCORE = 8
TC = 4096
D = 1024
NH = 16
DH = 64
CH = 64
NBAND = 8
BTB = 256
BT = 512
TPB = 4
CPB = 4

_DBG = {"upto": None, "outs": []}


class Em:
    ENG = dict(t="tensor", v="vector", s="scalar", y="sync", g="gpsimd")

    def __init__(self, nc, block, ctx):
        self.nc = nc
        self.block = block
        self.sem = {k: ctx.enter_context(nc.semaphore("sem_" + k)) for k in self.ENG}
        self.cnt = {k: 0 for k in self.ENG}
        self.lastw = {}

    def stage(self, ek, fns, waits=None, dma=False, selfdep=False, noinc=False):
        sem = self.sem
        lw = self.lastw
        wl = [(k, v) for k, v in (waits or {}).items() if k != ek and v > 0]
        own = self.cnt[ek]

        serial_dma = dma and ek == "g"
        def body(e):
            if (selfdep or dma) and own > 0 and lw.get((ek, ek), 0) < own:
                e.wait_ge(sem[ek], own)
                lw[(ek, ek)] = own
            for k, v in wl:
                if lw.get((ek, k), 0) >= v:
                    continue
                e.wait_ge(sem[k], v)
                lw[(ek, k)] = v
            run = own
            for i, f in enumerate(fns):
                if serial_dma and i > 0:
                    e.wait_ge(sem[ek], run)
                    lw[(ek, ek)] = run
                ins = f(e)
                if not noinc:
                    ins.then_inc(sem[ek], 16 if dma else 1)
                run += 16 if dma else 1

        getattr(self.block, self.ENG[ek])(body)
        if not noinc:
            self.cnt[ek] += (16 if dma else 1) * len(fns)
        return self.cnt[ek]

    def snap(self):
        return dict(self.cnt)

    def barrier(self):
        c = self.snap()
        for ek in self.ENG:
            w = {k: v for k, v in c.items() if k != ek}
            self.stage(ek, [lambda e: e.nop()], waits=w, noinc=True)


def build_nc(dbg_upto=None):
    nc = bass.Bass("TRN2", target_bir_lowering=False)
    ap = {}
    def din(name, shape, dt=F32):
        ap[name] = nc.dram_tensor(name, shape, dt, kind="ExternalInput").ap()
        return ap[name]

    x_h = din("x", [TC, D])
    wq_h = din("wqT", [D, D])
    wk_h = din("wkT", [D, D])
    wv_h = din("wvT", [D, D])
    wo_h = din("woT", [D, D])
    ctrl_h = din("ctrl", [D, 64])
    f1_h = din("f1w", [D, 4096])
    f2_h = din("f2w", [4096, D])
    biasT_h = din("biasT", [64, 64])
    araw_h = din("araw", [1, 16])
    braw_h = din("braw", [1, 16])
    ident_h = din("ident", [128, 128])
    maskj_h = din("maskj", [64, 64])
    masklo_h = din("masklo", [64, 64])
    maskup_h = din("maskup", [64, 64])
    tie_h = din("tie", [1, 128])
    gmask_h = din("gmask", [128, 8])

    out_h = nc.dram_tensor("out", [TC, D], F32, kind="ExternalOutput").ap()
    x1_d = nc.dram_tensor("x1d", [TC, D], BF16, kind="Internal").ap()
    inter_d = nc.dram_tensor("interd", [64 * 256, D], BF16, kind="Internal").ap()

    dbg_outs = []
    def dbg(name, shape, dt=BF16):
        a = nc.dram_tensor(name, shape, dt, kind="ExternalOutput").ap()
        dbg_outs.append(name)
        return a

    with ExitStack() as ctx:
        em_ctx = ctx
        ident_bf = ctx.enter_context(nc.sbuf_tensor("ident_bf", [128, 128], BF16))
        maskj_bf = ctx.enter_context(nc.sbuf_tensor("maskj_bf", [128, 64], BF16))
        mlo128 = ctx.enter_context(nc.sbuf_tensor("mlo128", [128, 64], F32))
        mup128 = ctx.enter_context(nc.sbuf_tensor("mup128", [128, 64], F32))
        mA0 = ctx.enter_context(nc.sbuf_tensor("mA0", [128, 1024], BF16))
        mB0 = ctx.enter_context(nc.sbuf_tensor("mB0", [128, 1024], BF16))
        abrow = ctx.enter_context(nc.sbuf_tensor("abrow", [1, 128], F32))
        abbc = ctx.enter_context(nc.sbuf_tensor("abbc", [128, 48], F32))
        ones1 = ctx.enter_context(nc.sbuf_tensor("ones1", [1, 128], F32))
        cst = ctx.enter_context(nc.sbuf_tensor("cst", [128, 4], F32))
        onescol = ctx.enter_context(nc.sbuf_tensor("onescol", [128, 1], BF16))
        tie_sb = ctx.enter_context(nc.sbuf_tensor("tie_sb", [1, 128], F32))
        ctrl_sb = ctx.enter_context(nc.sbuf_tensor("ctrl_sb", [128, 512], BF16))
        biasT_sb = ctx.enter_context(nc.sbuf_tensor("biasT_sb", [64, 64], F32))
        gmask_sb = ctx.enter_context(nc.sbuf_tensor("gmask_sb", [128, 8], BF16))

        block = ctx.enter_context(nc.Block())
        em = Em(nc, block, em_ctx)
        E = em.stage

        init_ctx = ExitStack()
        psIN = init_ctx.enter_context(nc.psum_tensor("psIN", [128, 512], F32))

        E("g", [
            lambda g: g.dma_start(out=ident_bf[:], in_=ident_h),
            lambda g: g.dma_start(out=maskj_bf[0:64, :], in_=maskj_h),
            lambda g: g.dma_start(out=maskj_bf[64:128, :], in_=maskj_h),
            lambda g: g.dma_start(out=mlo128[0:64, :], in_=masklo_h),
            lambda g: g.dma_start(out=mlo128[64:128, :], in_=masklo_h),
            lambda g: g.dma_start(out=mup128[0:64, :], in_=maskup_h),
            lambda g: g.dma_start(out=mup128[64:128, :], in_=maskup_h),
            lambda g: g.dma_start(out=abrow[:, 0:16], in_=araw_h),
            lambda g: g.dma_start(out=abrow[:, 16:32], in_=braw_h),
            lambda g: g.dma_start(out=tie_sb[:], in_=tie_h),
            lambda g: g.dma_start(
                out=ctrl_sb[:, :].rearrange("p (c n) -> p c n", c=8),
                in_=ctrl_h.rearrange("(c p) n -> p c n", p=128)),
            lambda g: g.dma_start(out=biasT_sb[:], in_=biasT_h),
            lambda g: g.dma_start(out=gmask_sb[:], in_=gmask_h),
        ], dma=True)
        g0 = em.cnt["g"]

        E("v", [lambda v: v.memset(ones1[:], 1.0),
                lambda v: v.memset(onescol[:], 1.0),
                lambda v: v.memset(cst[:, 0:1], 1e-5),
                lambda v: v.memset(cst[:, 1:2], 1e-24),
                lambda v: v.memset(cst[:, 3:4], 10.0)])
        E("v", [lambda v: v.reciprocal(cst[:, 2:3], cst[:, 3:4])], selfdep=True)
        E("s", [
            lambda s: s.activation(abrow[:, 32:48], abrow[:, 0:16], AF.Sigmoid),
            lambda s: s.activation(abrow[:, 48:64], abrow[:, 16:32], AF.Sigmoid),
        ], waits={"g": g0})
        E("v", [
            lambda v: v.tensor_scalar_mul(abrow[:, 64:80], abrow[:, 32:48], 0.5),
            lambda v: v.tensor_scalar_mul(abrow[:, 80:96], abrow[:, 48:64], 2.0),
        ], waits={"s": em.cnt["s"]})
        E("v", [
            lambda v: v.tensor_scalar_mul(abrow[:, 96:112], abrow[:, 64:80], -1.0),
            lambda v: v.reciprocal(abrow[:, 48:64], abrow[:, 80:96]),
        ], selfdep=True)
        E("v", [
            lambda v: v.tensor_tensor(abrow[:, 112:128], abrow[:, 96:112], abrow[:, 48:64], ALU.mult),
        ], selfdep=True)
        vrows = em.cnt["v"]
        E("t", [
            lambda t: t.matmul(psIN[:, 0:16], ones1[:], abrow[:, 80:96], start=True, stop=True),
            lambda t: t.matmul(psIN[:, 16:32], ones1[:], abrow[:, 112:128], start=True, stop=True),
            lambda t: t.matmul(psIN[:, 32:48], ones1[:], abrow[:, 96:112], start=True, stop=True),
        ], waits={"v": vrows})
        E("v", [lambda v: v.tensor_copy(abbc[:, 0:48], psIN[:, 0:48])], waits={"t": em.cnt["t"]})
        vbc = em.cnt["v"]
        mask_ops = []
        for h in range(NH):
            hr = (h % 2) * 8 + h // 2
            mask_ops.append(lambda v, h=h, hr=hr: v.tensor_scalar_mul(
                mA0[:, hr * 64:(hr + 1) * 64], mlo128[:], abbc[:, 32 + h:33 + h]))
            mask_ops.append(lambda v, h=h, hr=hr: v.tensor_scalar_mul(
                mB0[:, hr * 64:(hr + 1) * 64], mup128[:], abbc[:, 32 + h:33 + h]))
        E("v", mask_ops, waits={"g": g0}, selfdep=True)
        init_v = em.cnt["v"]
        init_ctx.close()

        with ExitStack() as pa:
            wq_sb = pa.enter_context(nc.sbuf_tensor("wq_sb", [128, 8192], BF16))
            wk_sb = pa.enter_context(nc.sbuf_tensor("wk_sb", [128, 8192], BF16))
            wv_sb = pa.enter_context(nc.sbuf_tensor("wv_sb", [128, 8192], BF16))
            wo_sb = pa.enter_context(nc.sbuf_tensor("wo_sb", [128, 8192], BF16))
            H_sb = pa.enter_context(nc.sbuf_tensor("H_sb", [128, 1024], BF16))
            M_sb = pa.enter_context(nc.sbuf_tensor("M_sb", [128, 1024], BF16))
            x_sb = pa.enter_context(nc.sbuf_tensor("x_sb", [128, 4096], F32))
            xr_sb = x_sb
            xn_tm = pa.enter_context(nc.sbuf_tensor("xn_tm", [128, 4096], BF16))
            xn_fm = pa.enter_context(nc.sbuf_tensor("xn_fm", [128, 4096], BF16))
            q_fm = pa.enter_context(nc.sbuf_tensor("q_fm", [128, 4096], BF16))
            k_fm = pa.enter_context(nc.sbuf_tensor("k_fm", [128, 4096], BF16))
            k_tm = pa.enter_context(nc.sbuf_tensor("k_tm", [128, 4096], BF16))
            kb_tm = pa.enter_context(nc.sbuf_tensor("kb_tm", [128, 4096], BF16))
            v_tm = pa.enter_context(nc.sbuf_tensor("v_tm", [128, 4096], BF16))
            o_tm = xn_tm
            o_fm = xn_fm
            x1_sb = pa.enter_context(nc.sbuf_tensor("x1_sb", [128, 4096], BF16))
            sq_sb = pa.enter_context(nc.sbuf_tensor("sq_sb", [128, 1024], BF16))
            st_sb = pa.enter_context(nc.sbuf_tensor("st_sb", [128, 64], F32))
            rk_sb = pa.enter_context(nc.sbuf_tensor("rk_sb", [128, 64], F32))
            A_bufs = [pa.enter_context(nc.sbuf_tensor(f"A{k}", [128, 1024], BF16)) for k in range(6)]
            B_bufs = [pa.enter_context(nc.sbuf_tensor(f"B{k}", [128, 1024], BF16)) for k in range(5)]
            W_fin = [pa.enter_context(nc.sbuf_tensor(f"Wf{b}", [128, 1024], BF16)) for b in range(2)]
            PT_sb = pa.enter_context(nc.sbuf_tensor("PT_sb", [128, 1024], BF16))
            RHS_sb = pa.enter_context(nc.sbuf_tensor("RHS_sb", [128, 1024], BF16))
            tmp_f = pa.enter_context(nc.sbuf_tensor("tmp_f", [128, 1024], BF16))
            re_sb = pa.enter_context(nc.sbuf_tensor("re_sb", [128, 16], F32))

            psA = pa.enter_context(nc.psum_tensor("psA", [128, 1024], F32))
            psB = pa.enter_context(nc.psum_tensor("psB", [128, 1024], F32))
            psC = pa.enter_context(nc.psum_tensor("psC", [128, 1024], F32))
            psD = pa.enter_context(nc.psum_tensor("psD", [128, 512], F32))
            psT = pa.enter_context(nc.psum_tensor("psT", [128, 1024], BF16))

            E("g", [
                lambda g: g.dma_start(out=wq_sb[:, :].rearrange("p (c n) -> p c n", c=8),
                                      in_=wq_h.rearrange("(c p) n -> p c n", p=128)),
                lambda g: g.dma_start(out=wk_sb[:, :].rearrange("p (c n) -> p c n", c=8),
                                      in_=wk_h.rearrange("(c p) n -> p c n", p=128)),
                lambda g: g.dma_start(out=wv_sb[:, :].rearrange("p (c n) -> p c n", c=8),
                                      in_=wv_h.rearrange("(c p) n -> p c n", p=128)),
                lambda g: g.dma_start(out=wo_sb[:, :].rearrange("p (c n) -> p c n", c=8),
                                      in_=wo_h.rearrange("(c p) n -> p c n", p=128)),
            ], dma=True)
            gW = em.cnt["g"]
            E("v", [lambda v: v.memset(H_sb[:], 0.0), lambda v: v.memset(M_sb[:], 0.0)])

            for band in range(NBAND):
                r0 = band * BTB
                r1 = 2048 + band * BTB
                yx = E("y", [
                    lambda y: y.dma_start(
                        out=x_sb[:, 0:2048].rearrange("p (n m) -> p n m", n=2),
                        in_=x_h[r0:r0 + 256, :].rearrange("(n p) m -> p n m", p=128)),
                    lambda y: y.dma_start(
                        out=x_sb[:, 2048:4096].rearrange("p (n m) -> p n m", n=2),
                        in_=x_h[r1:r1 + 256, :].rearrange("(n p) m -> p n m", p=128)),
                ], dma=True, waits={"v": em.cnt["v"], "t": em.cnt["t"], "s": em.cnt["s"]})

                for tl in range(TPB):
                    xt = x_sb[:, tl * 1024:(tl + 1) * 1024]
                    sS = E("s", [lambda s, xt=xt: s.activation(sq_sb[:], xt, AF.Square)],
                           waits={"y": yx, "v": em.cnt["v"]})
                    E("v", [
                        lambda v, xt=xt, tl=tl: v.reduce_sum(st_sb[:, tl * 8:tl * 8 + 1].unsqueeze(2),
                                                             xt.unsqueeze(1), axis=AX),
                    ], waits={"y": yx})
                    E("v", [
                        lambda v, tl=tl: v.reduce_sum(st_sb[:, tl * 8 + 1:tl * 8 + 2].unsqueeze(2),
                                                      sq_sb[:, :].unsqueeze(1), axis=AX),
                    ], waits={"s": sS})
                vred = em.cnt["v"]
                stm = st_sb[:, 32:36]; ste = st_sb[:, 36:40]; stv = st_sb[:, 40:44]
                str_ = st_sb[:, 44:48]; stmr = st_sb[:, 48:52]
                s1 = st_sb[:, 0:32].rearrange("p (t k) -> p t k", k=8)[:, :, 0]
                s2 = st_sb[:, 0:32].rearrange("p (t k) -> p t k", k=8)[:, :, 1]
                E("v", [
                    lambda v: v.tensor_scalar_mul(stm, s1, 1.0 / D),
                    lambda v: v.tensor_scalar_mul(ste, s2, 1.0 / D),
                ], selfdep=True)
                E("v", [lambda v: v.tensor_tensor(stv, stm, stm, ALU.mult)], selfdep=True)
                E("v", [lambda v: v.tensor_sub(stv, ste, stv)], selfdep=True)
                E("s", [lambda s: s.activation(str_, stv, AF.Ln, bias=cst[:, 0:1])],
                  waits={"v": em.cnt["v"]})
                sLN = E("s", [lambda s: s.activation(str_, str_, AF.Exp, scale=-0.5)],
                        selfdep=True)
                E("v", [lambda v: v.tensor_tensor(stmr, stm, str_, ALU.mult)],
                  waits={"s": sLN}, selfdep=True)
                for tl in range(TPB):
                    E("v", [lambda v, tl=tl: v.tensor_scalar(
                        xn_tm[:, tl * 1024:(tl + 1) * 1024], x_sb[:, tl * 1024:(tl + 1) * 1024],
                        str_[:, tl:tl + 1], stmr[:, tl:tl + 1], ALU.mult, ALU.subtract)],
                      waits={"t": em.cnt["t"]}, selfdep=(tl == 0))
                vxn = em.cnt["v"]

                tT0 = em.cnt["t"]
                for kc in range(8):
                    tt = E("t", [
                        lambda t, kc=kc, tl2=tl2: t.transpose(
                            psT[:, tl2 * 128:(tl2 + 1) * 128],
                            xn_tm[:, tl2 * 1024 + kc * 128:tl2 * 1024 + kc * 128 + 128],
                            ident_bf[:])
                        for tl2 in range(TPB)
                    ], waits={"v": vxn} if kc == 0 else {"v": em.cnt["v"]})
                    E("v", [lambda v, kc=kc: v.tensor_copy(
                        xn_fm[:, kc * 512:(kc + 1) * 512], psT[:, 0:512])],
                      waits={"t": tt})
                vxnf = em.cnt["v"]

                for oc in range(8):
                    tq = E("t", [
                        lambda t, oc=oc, kc2=kc2: t.matmul(
                            psA[:, 0:512],
                            wq_sb[:, kc2 * 1024 + oc * 128:kc2 * 1024 + oc * 128 + 128],
                            xn_fm[:, kc2 * 512:(kc2 + 1) * 512],
                            start=(kc2 == 0), stop=(kc2 == 7))
                        for kc2 in range(8)
                    ], waits=({"v": vxnf, "g": gW} if oc == 0 else {"v": em.cnt["v"]}))
                    E("v", [lambda v, oc=oc: v.tensor_copy(
                        q_fm[:, oc * 512:(oc + 1) * 512], psA[:, 0:512])], waits={"t": tq})
                vq = em.cnt["v"]

                for tl in range(TPB):
                    tkv = E("t", [
                        (lambda t, tl=tl, kc2=kc2, which=w, half=hf: t.matmul(
                            (psB if which == 0 else psC)[:, half * 512:half * 512 + 512],
                            xn_fm[:, kc2 * 512 + tl * 128:kc2 * 512 + tl * 128 + 128],
                            (wk_sb if which == 0 else wv_sb)[:, kc2 * 1024 + half * 512:kc2 * 1024 + half * 512 + 512],
                            start=(kc2 == 0), stop=(kc2 == 7)))
                        for kc2 in range(8) for w in (0, 1) for hf in (0, 1)
                    ], waits={"v": em.cnt["v"], "g": gW, "s": em.cnt["s"]})
                    E("s", [lambda s: s.activation(sq_sb[:], psB[:, 0:1024], AF.Square)],
                      waits={"t": tkv})
                    E("v", [lambda v, tl=tl: v.reduce_sum(
                        rk_sb[:, 32:48].unsqueeze(2),
                        sq_sb[:, :].rearrange("p (a b) -> p a b", b=64), axis=AX)],
                      waits={"s": em.cnt["s"]})
                    E("v", [lambda v: v.tensor_scalar_max(rk_sb[:, 32:48], rk_sb[:, 32:48], cst[:, 1:2])],
                      selfdep=True)
                    E("s", [lambda s: s.activation(rk_sb[:, 0:16], rk_sb[:, 32:48], AF.Ln)],
                      waits={"v": em.cnt["v"]})
                    sRK = E("s", [lambda s: s.activation(rk_sb[:, 0:16], rk_sb[:, 0:16], AF.Exp, scale=-0.5)],
                            selfdep=True)
                    E("v", [lambda v: v.tensor_tensor(rk_sb[:, 16:32], rk_sb[:, 0:16], abbc[:, 0:16], ALU.mult)],
                      waits={"s": sRK})
                    _rkb = em.cnt["v"]
                    E("v", [
                        lambda v, tl=tl: v.tensor_tensor(
                            k_tm[:, tl * 1024:(tl + 1) * 1024].rearrange("p (a b) -> p a b", b=64),
                            psB[:, 0:1024].rearrange("p (a b) -> p a b", b=64),
                            rk_sb[:, 0:16].unsqueeze(2).broadcast_to([128, 16, 64]),
                            ALU.mult),
                        lambda v, tl=tl: v.tensor_tensor(
                            kb_tm[:, tl * 1024:(tl + 1) * 1024].rearrange("p (a b) -> p a b", b=64),
                            psB[:, 0:1024].rearrange("p (a b) -> p a b", b=64),
                            rk_sb[:, 16:32].unsqueeze(2).broadcast_to([128, 16, 64]),
                            ALU.mult),
                    ], selfdep=True)
                    E("s", [lambda s, tl=tl: s.activation(
                        v_tm[:, tl * 1024:(tl + 1) * 1024], psC[:, 0:1024], AF.Copy)])
                vkv = em.cnt["v"]
                skv = em.cnt["s"]

                for oc in range(8):
                    tt = E("t", [
                        lambda t, oc=oc, tl2=tl2: t.transpose(
                            psT[:, tl2 * 128:(tl2 + 1) * 128],
                            k_tm[:, tl2 * 1024 + oc * 128:tl2 * 1024 + oc * 128 + 128],
                            ident_bf[:])
                        for tl2 in range(TPB)
                    ], waits={"v": vkv if oc == 0 else em.cnt["v"], "s": skv})
                    E("v", [lambda v, oc=oc: v.tensor_copy(
                        k_fm[:, oc * 512:(oc + 1) * 512], psT[:, 0:512])], waits={"t": tt})
                vkf = em.cnt["v"]

                if dbg_upto == "qkv" and band == 0:
                    _emit_dbg_qkv(nc, em, E, locals(), dbg)
                    break

                for cp in range(2):
                    for b in range(2):
                        gfns = []
                        for h in range(NH):
                            hb = (h % 2) * 64
                            hr = (h % 2) * 8 + h // 2
                            col = (h // 2) * 512 + b * 256
                            for ci in range(2):
                                c = 2 * cp + ci
                                ksl = k_fm[hb:hb + 64, col + c * 64:col + c * 64 + 64]
                                gfns.append(lambda t, ksl=ksl, hr=hr, h=h, ci=ci: t.matmul(
                                    psA[ci * 64:ci * 64 + 64, hr * 64:hr * 64 + 64],
                                    ksl, ksl, start=True, stop=True,
                                    tile_position=((h % 2) * 64, ci * 64)))
                        tG = E("t", gfns, waits={"v": em.cnt["v"], "s": em.cnt["s"]})
                        vM = E("v", [
                            lambda v: v.tensor_tensor(A_bufs[0][:], psA[:, 0:1024], mA0[:], ALU.mult),
                            lambda v: v.tensor_tensor(B_bufs[0][:], psA[:, 0:1024], mB0[:], ALU.mult),
                        ], waits={"t": tG, "v": init_v})
                        if dbg_upto == "gmask":
                            break
                        for k in range(5):
                            sq_fns = []
                            for h in range(NH):
                                for par in (0, 64):
                                    sq_fns.append(lambda t, k=k, h=h, par=par: t.matmul(
                                        psB[par:par + 64, h * 64:h * 64 + 64],
                                        B_bufs[k][par:par + 64, h * 64:h * 64 + 64],
                                        A_bufs[k][par:par + 64, h * 64:h * 64 + 64],
                                        start=True, stop=True, tile_position=(par, par)))
                                    if k < 4:
                                        sq_fns.append(lambda t, k=k, h=h, par=par: t.matmul(
                                            psC[par:par + 64, h * 64:h * 64 + 64],
                                            A_bufs[k][par:par + 64, h * 64:h * 64 + 64],
                                            B_bufs[k][par:par + 64, h * 64:h * 64 + 64],
                                            start=True, stop=True, tile_position=(par, par)))
                            tS = E("t", sq_fns, waits={"v": vM, "s": em.cnt["s"]})
                            if k < 4:
                                E("s", [lambda s, k=k: s.copy(B_bufs[k + 1][:], psC[:, 0:1024])],
                                  waits={"t": tS})
                            vM = E("v", [lambda v, k=k: v.tensor_copy(A_bufs[k + 1][:], psB[:, 0:1024])],
                                   waits={"t": tS})
                        kb_cols = [None, None]
                        for ci in range(2):
                            c = 2 * cp + ci
                            tl = b * 2 + c // 2
                            kb_cols[ci] = tl * 1024
                        tlp = b * 2 + cp
                        kbt = kb_tm[:, tlp * 1024:(tlp + 1) * 1024]
                        Wcur = None
                        for k in range(6):
                            ap_fns = []
                            for h in range(NH):
                                hr = (h % 2) * 8 + h // 2
                                rsl = (kbt[:, h * 64:h * 64 + 64] if k == 0
                                       else Wcur[:, hr * 64:hr * 64 + 64])
                                for par in (0, 64):
                                    ap_fns.append(lambda t, k=k, hr=hr, par=par, rsl=rsl: t.matmul(
                                        psB[par:par + 64, hr * 64:hr * 64 + 64],
                                        A_bufs[k][par:par + 64, hr * 64:hr * 64 + 64],
                                        rsl[par:par + 64, :],
                                        start=True, stop=True, tile_position=(par, par)))
                            tA = E("t", ap_fns, waits={"v": vM, "s": em.cnt["s"]})
                            Wnext = W_fin[b] if k == 5 else A_bufs[k]
                            if k == 0:
                                vM = E("v", [lambda v, Wnext=Wnext, kbt=kbt: v.tensor_tensor(
                                    Wnext[:, :].rearrange("p (b a c) -> p b a c", b=2, a=8),
                                    psB[:, 0:1024].rearrange("p (b a c) -> p b a c", b=2, a=8),
                                    kbt.rearrange("p (a b c) -> p b a c", a=8, b=2),
                                    ALU.add)], waits={"t": tA}, selfdep=True)
                            else:
                                vM = E("v", [lambda v, Wnext=Wnext, Wcur=Wcur: v.tensor_tensor(
                                    Wnext[:], psB[:, 0:1024], Wcur[:], ALU.add)],
                                    waits={"t": tA}, selfdep=True)
                            Wcur = Wnext
                        if dbg_upto == "solve" and cp == 0 and b == 0:
                            break
                    vSolve = em.cnt["v"]
                    if dbg_upto in ("solve", "gmask") and cp == 0:
                        break

                    for ci in range(2):
                        c = 2 * cp + ci
                        cb = ci * 64
                        pbase = (c % 2) * 64
                        for b in range(2):
                            tl = b * 2 + c // 2
                            fns = []
                            for h in range(NH):
                                hb = (h % 2) * 64
                                hr = (h % 2) * 8 + h // 2
                                fcol = (h // 2) * 512 + b * 256 + c * 64
                                ksl = k_fm[hb:hb + 64, fcol:fcol + 64]
                                qsl = q_fm[hb:hb + 64, fcol:fcol + 64]
                                fns.append(lambda t, ksl=ksl, qsl=qsl, hr=hr, h=h, pbase=pbase: t.matmul(
                                    psA[pbase:pbase + 64, hr * 64:hr * 64 + 64], ksl, qsl,
                                    start=True, stop=True, tile_position=((h % 2) * 64, pbase)))
                                fns.append(lambda t, ksl=ksl, hr=hr, h=h, b=b, pbase=pbase: t.matmul(
                                    psB[pbase:pbase + 64, hr * 64:hr * 64 + 64], ksl,
                                    H_sb[(h % 2) * 64:(h % 2) * 64 + 64, b * 512 + (h // 2) * 64:b * 512 + (h // 2) * 64 + 64],
                                    start=True, stop=True, tile_position=((h % 2) * 64, pbase)))
                                fns.append(lambda t, qsl=qsl, hr=hr, h=h, b=b, pbase=pbase: t.matmul(
                                    psC[pbase:pbase + 64, hr * 64:hr * 64 + 64], qsl,
                                    M_sb[(h % 2) * 64:(h % 2) * 64 + 64, b * 512 + (h // 2) * 64:b * 512 + (h // 2) * 64 + 64],
                                    start=True, stop=True, tile_position=((h % 2) * 64, pbase)))
                            tSMm = E("t", fns, waits={"v": em.cnt["v"]})
                            sE = E("s", [lambda s, pbase=pbase: s.activation(
                                PT_sb[pbase:pbase + 64, :], psA[pbase:pbase + 64, 0:1024],
                                AF.Exp, scale=0.125)], waits={"t": tSMm})
                            vP = E("v", [
                                lambda v, pbase=pbase: v.tensor_tensor(
                                    PT_sb[pbase:pbase + 64, :].rearrange("p (a b) -> p a b", b=64),
                                    PT_sb[pbase:pbase + 64, :].rearrange("p (a b) -> p a b", b=64),
                                    maskj_bf[pbase:pbase + 64, :].unsqueeze(1).broadcast_to([64, 16, 64]),
                                    ALU.mult),
                                lambda v, pbase=pbase: v.tensor_tensor(
                                    tmp_f[pbase:pbase + 64, :].rearrange("p (b a c) -> p b a c", b=2, a=8),
                                    psB[pbase:pbase + 64, 0:1024].rearrange("p (b a c) -> p b a c", b=2, a=8),
                                    abbc[pbase:pbase + 64, 16:32].rearrange("p (a b) -> p b a", a=8)
                                        .unsqueeze(3).broadcast_to([64, 2, 8, 64]),
                                    ALU.mult),
                            ], waits={"s": sE, "t": tSMm})
                            vR = E("v", [
                                lambda v, tl=tl, pbase=pbase: v.tensor_add(
                                    RHS_sb[pbase:pbase + 64, :].rearrange("p (b a c) -> p b a c", b=2, a=8),
                                    tmp_f[pbase:pbase + 64, :].rearrange("p (b a c) -> p b a c", b=2, a=8),
                                    v_tm[pbase:pbase + 64, tl * 1024:(tl + 1) * 1024]
                                        .rearrange("p (a b c) -> p b a c", a=8, b=2)),
                            ], selfdep=True)
                            fns = []
                            for h in range(NH):
                                hr = (h % 2) * 8 + h // 2
                                psl = PT_sb[pbase:pbase + 64, hr * 64:hr * 64 + 64]
                                fns.append(lambda t, psl=psl, hr=hr, pbase=pbase: t.matmul(
                                    psD[pbase:pbase + 64, hr:hr + 1], psl, onescol[pbase:pbase + 64, :],
                                    start=True, stop=True, tile_position=(pbase, pbase)))
                                fns.append(lambda t, psl=psl, hr=hr, h=h, tl=tl, pbase=pbase: t.matmul(
                                    psA[pbase:pbase + 64, hr * 64:hr * 64 + 64], psl,
                                    v_tm[pbase:pbase + 64, tl * 1024 + h * 64:tl * 1024 + h * 64 + 64],
                                    start=True, stop=True, tile_position=(pbase, pbase)))
                            tPV = E("t", fns, waits={"v": vP, "s": sE})
                            fns = []
                            for h in range(NH):
                                hr = (h % 2) * 8 + h // 2
                                wsl = W_fin[b][pbase:pbase + 64, hr * 64:hr * 64 + 64]
                                rsl = RHS_sb[pbase:pbase + 64, hr * 64:hr * 64 + 64]
                                fns.append(lambda t, wsl=wsl, rsl=rsl, h=h, pbase=pbase: t.matmul(
                                    psB[(h % 2) * 64:(h % 2) * 64 + 64, (h // 2) * 64:(h // 2) * 64 + 64],
                                    wsl, rsl, start=True, stop=True, tile_position=(pbase, (h % 2) * 64)))
                                fns.append(lambda t, wsl=wsl, rsl=rsl, h=h, pbase=pbase: t.matmul(
                                    psB[(h % 2) * 64:(h % 2) * 64 + 64, 512 + (h // 2) * 64:512 + (h // 2) * 64 + 64],
                                    rsl, wsl, start=True, stop=True, tile_position=(pbase, (h % 2) * 64)))
                            tHM = E("t", fns, waits={"v": vR})
                            vRe = E("v", [lambda v, pbase=pbase: v.reciprocal(
                                re_sb[pbase:pbase + 64, 0:16], psD[pbase:pbase + 64, 0:16])],
                                waits={"t": tPV})
                            E("v", [
                                lambda v, pbase=pbase: v.tensor_tensor(
                                    tmp_f[pbase:pbase + 64, :].rearrange("p (a b) -> p a b", b=64),
                                    psA[pbase:pbase + 64, 0:1024].rearrange("p (a b) -> p a b", b=64),
                                    re_sb[pbase:pbase + 64, 0:16].unsqueeze(2).broadcast_to([64, 16, 64]),
                                    ALU.mult),
                            ], selfdep=True)
                            E("v", [
                                lambda v, tl=tl, pbase=pbase: v.tensor_add(
                                    o_tm[pbase:pbase + 64, tl * 1024:(tl + 1) * 1024]
                                        .rearrange("p (a b c) -> p b a c", a=8, b=2),
                                    tmp_f[pbase:pbase + 64, :]
                                        .rearrange("p (b a c) -> p b a c", b=2, a=8),
                                    psC[pbase:pbase + 64, 0:1024]
                                        .rearrange("p (b a c) -> p b a c", b=2, a=8)),
                            ], selfdep=True)
                            E("v", [
                                lambda v, b=b: v.tensor_add(
                                    H_sb[:, b * 512:(b + 1) * 512],
                                    H_sb[:, b * 512:(b + 1) * 512],
                                    psB[:, 0:512]),
                                lambda v, b=b: v.scalar_tensor_tensor(
                                    M_sb[:, b * 512:(b + 1) * 512],
                                    psB[:, 512:1024], cst[:, 2:3],
                                    M_sb[:, b * 512:(b + 1) * 512],
                                    ALU.mult, ALU.add),
                            ], waits={"t": tHM})

                if dbg_upto in ("solve", "gmask") and band == 0:
                    _emit_dbg_solve(nc, em, E, locals(), dbg)
                    break
                if dbg_upto == "chain" and band == 0:
                    _emit_dbg_chain(nc, em, E, locals(), dbg)
                    break

                vO = em.cnt["v"]
                for oc in range(8):
                    tt = E("t", [
                        lambda t, oc=oc, tl2=tl2: t.transpose(
                            psT[:, tl2 * 128:(tl2 + 1) * 128],
                            o_tm[:, tl2 * 1024 + oc * 128:tl2 * 1024 + oc * 128 + 128],
                            ident_bf[:])
                        for tl2 in range(TPB)
                    ], waits={"v": vO if oc == 0 else em.cnt["v"]})
                    E("v", [lambda v, oc=oc: v.tensor_copy(
                        o_fm[:, oc * 512:(oc + 1) * 512], psT[:, 0:512])], waits={"t": tt})
                vOf = em.cnt["v"]
                yxr = em.cnt["y"]
                for tl in range(TPB):
                    tW = E("t", [
                        (lambda t, tl=tl, oc2=oc2, half=hf: t.matmul(
                            psA[:, half * 512:half * 512 + 512],
                            o_fm[:, oc2 * 512 + tl * 128:oc2 * 512 + tl * 128 + 128],
                            wo_sb[:, oc2 * 1024 + half * 512:oc2 * 1024 + half * 512 + 512],
                            start=(oc2 == 0), stop=(oc2 == 7)))
                        for oc2 in range(8) for hf in (0, 1)
                    ], waits={"v": em.cnt["v"]})
                    E("v", [lambda v, tl=tl: v.tensor_add(
                        x1_sb[:, tl * 1024:(tl + 1) * 1024],
                        xr_sb[:, tl * 1024:(tl + 1) * 1024],
                        psA[:, 0:1024])], waits={"t": tW})
                vx1 = em.cnt["v"]
                E("y", [
                    lambda y: y.dma_start(
                        out=x1_d[r0:r0 + 256, :].rearrange("(n p) m -> p n m", p=128),
                        in_=x1_sb[:, 0:2048].rearrange("p (n m) -> p n m", n=2)),
                    lambda y: y.dma_start(
                        out=x1_d[r1:r1 + 256, :].rearrange("(n p) m -> p n m", p=128),
                        in_=x1_sb[:, 2048:4096].rearrange("p (n m) -> p n m", n=2)),
                ], dma=True, waits={"v": vx1})
                if dbg_upto == "band1":
                    break

            em.barrier()

        if dbg_upto in ("qkv", "chain", "solve", "gmask"):
            _finalize_wait(em, E)
            return nc, dbg_outs
        if dbg_upto in ("phasea", "band1"):
            a = nc.dram_tensor("dbg_x1", [TC, D], BF16, kind="ExternalOutput").ap()
            dbg_outs.append("dbg_x1")
            E("y", [lambda y: y.dma_start(out=a, in_=x1_d)], dma=True, waits=em.snap())
            _finalize_wait(em, E)
            return nc, dbg_outs

        with ExitStack() as pb:
            sel_sb = pb.enter_context(nc.sbuf_tensor("sel_sb", [128, 65536], BF16))
            pselT_sb = pb.enter_context(nc.sbuf_tensor("pselT_sb", [128, 8192], BF16))
            psel_sb = pb.enter_context(nc.sbuf_tensor("psel_sb", [128, 512], BF16))
            x1t_sb = pb.enter_context(nc.sbuf_tensor("x1t_sb", [128, 1024], BF16))
            y_sb = pb.enter_context(nc.sbuf_tensor("y_sb", [128, 1024], BF16))
            yfbuf = pb.enter_context(nc.sbuf_tensor("yfbuf", [128, 2048], BF16))
            yf_sb = yfbuf[:, 0:1024]
            sqb_sb = yf_sb
            stb_sb = pb.enter_context(nc.sbuf_tensor("stb_sb", [128, 16], F32))
            perm_sb = pb.enter_context(nc.sbuf_tensor("perm_sb", [64, 128], BF16))
            permT_sb = pb.enter_context(nc.sbuf_tensor("permT_sb", [128, 64], BF16))
            mx_sb = pb.enter_context(nc.sbuf_tensor("mx_sb", [64, 8], F32))
            f1a = pb.enter_context(nc.sbuf_tensor("f1a", [128, 4096], BF16))
            f1_sb = [f1a[:, 0:2048], pb.enter_context(nc.sbuf_tensor("f1_1", [128, 2048], BF16))[:, :]]
            f2_sb = [pb.enter_context(nc.sbuf_tensor(f"f2_{i}", [64, 4096], BF16)) for i in range(2)]
            inner_sb2 = [pb.enter_context(nc.sbuf_tensor(f"inner_{i}", [64, 128], BF16)) for i in range(2)]
            scat_sb = f1a
            inter_sb = pb.enter_context(nc.sbuf_tensor("inter_sb", [128, 4096], BF16))
            outf_sb = yfbuf[:, :].bitcast(F32)

            pb0 = pb.enter_context(nc.psum_tensor("pb0", [128, 512], F32))
            pb1 = pb.enter_context(nc.psum_tensor("pb1", [128, 512], F32))
            pb2 = pb.enter_context(nc.psum_tensor("pb2", [128, 512], F32))
            pb3 = pb.enter_context(nc.psum_tensor("pb3", [128, 1024], F32))
            pb4 = pb.enter_context(nc.psum_tensor("pb4", [128, 1024], F32))
            pbT = pb.enter_context(nc.psum_tensor("pbT", [128, 1024], BF16))

            for half in range(2):
                for tl in range(16):
                    gt = half * 16 + tl
                    row = gt * 128
                    yx1 = E("y", [lambda y, row=row: y.dma_start(
                        out=x1t_sb[:], in_=x1_d[row:row + 128, :])],
                        dma=True, waits={"t": em.cnt["t"], "v": em.cnt["v"], "s": em.cnt["s"]})
                    sS = E("s", [lambda s: s.activation(sqb_sb, x1t_sb[:], AF.Square)],
                           waits={"y": yx1})
                    E("v", [lambda v: v.reduce_sum(stb_sb[:, 0:1].unsqueeze(2),
                                                   x1t_sb[:, :].unsqueeze(1), axis=AX)],
                      waits={"y": yx1})
                    E("v", [lambda v: v.reduce_sum(stb_sb[:, 1:2].unsqueeze(2),
                                                   sqb_sb.unsqueeze(1), axis=AX)],
                      waits={"s": sS})
                    E("v", [
                        lambda v: v.tensor_scalar_mul(stb_sb[:, 2:3], stb_sb[:, 0:1], 1.0 / D),
                        lambda v: v.tensor_scalar_mul(stb_sb[:, 3:4], stb_sb[:, 1:2], 1.0 / D),
                    ], selfdep=True)
                    E("v", [lambda v: v.tensor_tensor(stb_sb[:, 4:5], stb_sb[:, 2:3], stb_sb[:, 2:3], ALU.mult)], selfdep=True)
                    E("v", [lambda v: v.tensor_sub(stb_sb[:, 4:5], stb_sb[:, 3:4], stb_sb[:, 4:5])], selfdep=True)
                    E("s", [lambda s: s.activation(stb_sb[:, 5:6], stb_sb[:, 4:5], AF.Ln, bias=cst[:, 0:1])],
                      waits={"v": em.cnt["v"]})
                    sLN = E("s", [lambda s: s.activation(stb_sb[:, 5:6], stb_sb[:, 5:6], AF.Exp, scale=-0.5)],
                            selfdep=True)
                    E("v", [lambda v: v.tensor_tensor(stb_sb[:, 6:7], stb_sb[:, 2:3], stb_sb[:, 5:6], ALU.mult)],
                      waits={"s": sLN}, selfdep=True)
                    vy = E("v", [lambda v: v.tensor_scalar(
                        y_sb[:], x1t_sb[:], stb_sb[:, 5:6], stb_sb[:, 6:7], ALU.mult, ALU.subtract)],
                        selfdep=True)
                    tT = E("t", [
                        lambda t, kc=kc: t.transpose(
                            pbT[:, kc * 128:(kc + 1) * 128],
                            y_sb[:, kc * 128:(kc + 1) * 128], ident_bf[:])
                        for kc in range(8)
                    ], waits={"v": vy})
                    vyf = E("v", [lambda v: v.tensor_copy(yf_sb, pbT[:, 0:1024])], waits={"t": tT})
                    tL = E("t", [
                        lambda t, kc=kc: t.matmul(
                            pb0[0:64, 0:128],
                            ctrl_sb[:, kc * 64:(kc + 1) * 64],
                            yf_sb[:, kc * 128:(kc + 1) * 128],
                            start=(kc == 0), stop=False)
                        for kc in range(8)
                    ] + [lambda t: t.matmul(
                        pb0[0:64, 0:128], ones1[:, 0:64], tie_sb[:, 0:128],
                        start=False, stop=True)], waits={"v": vyf})
                    vmx = E("v", [lambda v: v.reduce_max(
                        mx_sb[:, :].unsqueeze(2),
                        pb0[0:64, 0:128].rearrange("p (a b) -> p a b", b=16), axis=AX)],
                        waits={"t": tL})
                    vpm = E("v", [lambda v: v.tensor_tensor(
                        perm_sb[:, :].rearrange("p (a b) -> p a b", b=16),
                        pb0[0:64, 0:128].rearrange("p (a b) -> p a b", b=16),
                        mx_sb[:, :].unsqueeze(2).broadcast_to([64, 8, 16]),
                        ALU.is_equal)], selfdep=True)
                    tPT = E("t", [lambda t: t.transpose(
                        pbT[:, 0:64], perm_sb[:], ident_bf[0:64, 0:64])], waits={"v": vpm})
                    E("v", [lambda v: v.tensor_copy(permT_sb[:], pbT[:, 0:64])],
                      waits={"t": tPT}, selfdep=True)
                    vPS = E("v", [lambda v: v.tensor_tensor(
                        psel_sb[:, :].rearrange("p (e g) -> p e g", g=8),
                        permT_sb[:, :].unsqueeze(2).broadcast_to([128, 64, 8]),
                        gmask_sb[:, :].unsqueeze(1).broadcast_to([128, 64, 8]),
                        ALU.mult)], selfdep=True)
                    tPs = E("t", [
                        lambda t, k=k: t.transpose(
                            pbT[:, k * 128:(k + 1) * 128],
                            psel_sb[:, k * 128:(k + 1) * 128], ident_bf[:])
                        for k in range(4)
                    ], waits={"v": vPS})
                    E("v", [lambda v, tl=tl: v.tensor_copy(
                        pselT_sb[:, tl * 512:(tl + 1) * 512], pbT[:, 0:512])], waits={"t": tPs})
                    for kc in range(8):
                        pg = pb1 if kc % 2 == 0 else pb2
                        tg = E("t", [lambda t, kc=kc, pg=pg: t.matmul(
                            pg[:, 0:512], y_sb[:, kc * 128:(kc + 1) * 128], psel_sb[:],
                            start=True, stop=True)],
                            waits={"v": em.cnt["v"], "s": em.cnt["s"]})
                        ek2 = "v" if kc % 2 == 0 else "s"
                        if kc % 2 == 0:
                            E("v", [lambda v, kc=kc, tl=tl, pg=pg: v.tensor_copy(
                                sel_sb[:, (kc * 16 + tl) * 512:(kc * 16 + tl) * 512 + 512],
                                pg[:, 0:512])], waits={"t": tg})
                        else:
                            E("s", [lambda s, kc=kc, tl=tl, pg=pg: s.copy(
                                sel_sb[:, (kc * 16 + tl) * 512:(kc * 16 + tl) * 512 + 512],
                                pg[:, 0:512])], waits={"t": tg})
                vGather = em.cnt["v"]

                def f1_load(g, j, buf):
                    return g.dma_start(
                        out=f1_sb[buf].rearrange("p (c F) -> p c F", c=8),
                        in_=f1_h[:, j * 256:(j + 1) * 256].rearrange("(c p) F -> p c F", p=128))

                def f2_load(g, j, buf):
                    return g.dma_start(
                        out=f2_sb[buf][:, :].rearrange("p (el m) -> p el m", el=4),
                        in_=f2_h[j * 256:(j + 1) * 256, :].rearrange("(el p) m -> p el m", p=64))

                gF = E("g", [lambda g: f1_load(g, 0, 0), lambda g: f2_load(g, 0, 0)],
                       dma=True, waits={"t": em.cnt["t"]})
                relu_cnt = {0: em.cnt["s"], 1: em.cnt["s"]}
                inter_cnt = {0: em.cnt["v"], 1: em.cnt["v"]}
                for j in range(16):
                    buf = j % 2
                    if j < 15:
                        gF = E("g", [lambda g, j=j, buf=buf: f1_load(g, j + 1, 1 - buf),
                                     lambda g, j=j, buf=buf: f2_load(g, j + 1, 1 - buf)],
                               dma=True, waits={"t": em.cnt["t"]})
                        gw = gF - 32
                    else:
                        gw = gF
                    for el in range(4):
                        es = j * 4 + el
                        par = es % 2
                        pi = pb1 if par == 0 else pb2
                        po5 = pb3 if par == 0 else pb4
                        tF1 = E("t", [
                            lambda t, kc=kc, el=el, buf=buf, pi=pi: t.matmul(
                                pi[0:64, 0:128],
                                f1_sb[buf][:, kc * 256 + el * 64:kc * 256 + el * 64 + 64],
                                sel_sb[:, :].rearrange("p (c l g) -> p c l g", c=8, l=16)[:, kc, :, :]
                                      .rearrange("p l (e g) -> p l e g", g=8)[:, :, es, :],
                                start=(kc == 0), stop=(kc == 7))
                            for kc in range(8)
                        ], waits={"v": vGather, "g": gw,
                                  "s": relu_cnt.get(par, 0)})
                        sR = E("s", [lambda s, es=es, pi=pi, par=par: s.activation(
                            inner_sb2[par][:], pi[0:64, 0:128], AF.Relu,
                            bias=biasT_sb[:, es:es + 1])],
                            waits={"t": tF1})
                        relu_cnt[par] = sR
                        tF2 = E("t", [
                            lambda t, hf=hf, el=el, buf=buf, po5=po5, par=par: t.matmul(
                                po5[:, hf * 512:hf * 512 + 512], inner_sb2[par][:],
                                f2_sb[buf][:, el * 1024 + hf * 512:el * 1024 + hf * 512 + 512],
                                start=True, stop=True)
                            for hf in (0, 1)
                        ], waits={"s": sR, "v": inter_cnt.get(par, 0)})
                        E("v", [lambda v, el=el, po5=po5: v.tensor_copy(
                            inter_sb[:, el * 1024:(el + 1) * 1024], po5[:, 0:1024])],
                          waits={"t": tF2, "y": em.cnt["y"]})
                        inter_cnt[par] = em.cnt["v"]
                    vI8 = em.cnt["v"]
                    e0 = (j % 4) * 4
                    kblk = j // 4
                    E("y", [lambda y, e0=e0, kblk=kblk, half=half: y.dma_start(
                        out=inter_d.rearrange("(e R k) m -> e R k m", e=16, R=256)[
                            e0:e0 + 4, half * 128:half * 128 + 128, kblk, :].transpose([1, 0, 2]),
                        in_=inter_sb[:, :].rearrange("p (el m) -> p el m", el=4))],
                      dma=True, waits={"v": vI8})
                yInter = em.cnt["y"]

                for tl in range(16):
                    gt = half * 16 + tl
                    row = gt * 128
                    ysc = E("y", [
                        lambda y, tl=tl, half=half: y.dma_start(
                            out=scat_sb[:, :],
                            in_=inter_d.rearrange("(e R k) m -> e R (k m)", e=16, R=256)[
                                :, half * 128 + tl * 8:half * 128 + tl * 8 + 8, :]),
                        lambda y, row=row: y.dma_start(out=x1t_sb[:], in_=x1_d[row:row + 128, :]),
                    ], dma=True, waits={"y": yInter, "v": em.cnt["v"], "t": em.cnt["t"]})
                    po = pb3 if tl % 2 == 0 else pb4
                    tSc = E("t", [
                        (lambda t, k=k, hf=hf, tl=tl, po=po: t.matmul(
                            po[:, hf * 512:hf * 512 + 512],
                            pselT_sb[:, tl * 512 + k * 128:tl * 512 + k * 128 + 128],
                            scat_sb[:, k * 1024 + hf * 512:k * 1024 + hf * 512 + 512],
                            start=(k == 0), stop=(k == 3)))
                        for k in range(4) for hf in (0, 1)
                    ], waits={"y": ysc})
                    vF = E("v", [lambda v, po=po: v.tensor_add(outf_sb, x1t_sb[:], po[:, 0:1024])],
                           waits={"t": tSc, "y": ysc})
                    E("y", [lambda y, row=row: y.dma_start(out=out_h[row:row + 128, :], in_=outf_sb)],
                      dma=True, waits={"v": vF})

        _finalize_wait(em, E)
    return nc, dbg_outs


def _finalize_wait(em, E):
    E("t", [lambda t: t.nop()], waits=em.snap(), noinc=True)


def _emit_dbg_qkv(nc, em, E, L, dbg):
    outs = {}
    for nm in ("q_fm", "k_tm", "kb_tm", "v_tm", "k_fm", "xn_tm", "xn_fm"):
        outs[nm] = dbg("dbg_" + nm, [128, 4096])
    w = em.snap()
    E("y", [
        (lambda y, a=outs[nm], t=L[nm]: y.dma_start(out=a, in_=t[:, :]))
        for nm in outs
    ], dma=True, waits=w)


def _emit_dbg_solve(nc, em, E, L, dbg):
    outs = {}
    outs["W_fin0"] = dbg("dbg_W_fin0", [128, 1024])
    outs["A0"] = dbg("dbg_A0", [128, 1024])
    outs["B0"] = dbg("dbg_B0", [128, 1024])
    w = em.snap()
    E("y", [lambda y, a=outs["W_fin0"], t=L["W_fin"][0]: y.dma_start(out=a, in_=t[:, :]),
            lambda y, a=outs["A0"], t=L["A_bufs"][0]: y.dma_start(out=a, in_=t[:, :]),
            lambda y, a=outs["B0"], t=L["A_bufs"][1]: y.dma_start(out=a, in_=t[:, :])],
      dma=True, waits=w)


def _emit_dbg_chain(nc, em, E, L, dbg):
    outs = {}
    for nm in ("o_tm", "q_fm", "k_fm", "v_tm"):
        outs[nm] = dbg("dbg_" + nm, [128, 4096])
    for nm in ("H_sb", "M_sb"):
        outs[nm] = dbg("dbg_" + nm, [128, 1024])
    w = em.snap()
    E("y", [
        (lambda y, a=outs[nm], t=L[nm]: y.dma_start(out=a, in_=t[:, :]))
        for nm in outs
    ], dma=True, waits=w)



def _consts():
    ident = np.eye(128, dtype=np.float32)
    maskj = np.triu(np.ones((64, 64), np.float32))
    masklo = np.tril(np.ones((64, 64), np.float32), -1)
    maskup = np.triu(np.ones((64, 64), np.float32), 1)
    tie = np.tile(np.linspace(0.0, 1e-6, 16, dtype=np.float32), 8)[None, :]
    gmask = (np.arange(128)[:, None] // 16 == np.arange(8)[None, :]).astype(np.float32)
    return ident, maskj, masklo, maskup, tie, gmask


def kernel(L0_attn_x, L0_attn_Wq, L0_attn_Wk, L0_attn_Wv, L0_attn_Wo,
           L0_attn_alpha_raw, L0_attn_beta_raw,
           L0_ffn_controller, L0_ffn_f1, L0_ffn_f2, L0_ffn_bias):
    x = np.ascontiguousarray(np.asarray(L0_attn_x, np.float32))
    wqT = np.ascontiguousarray(np.asarray(L0_attn_Wq, np.float32).T)
    wkT = np.ascontiguousarray(np.asarray(L0_attn_Wk, np.float32).T)
    wvT = np.ascontiguousarray(np.asarray(L0_attn_Wv, np.float32).T)
    woT = np.ascontiguousarray(np.asarray(L0_attn_Wo, np.float32).T)
    ctrl = np.ascontiguousarray(np.asarray(L0_ffn_controller, np.float32).reshape(D, 64))
    f1w = np.ascontiguousarray(np.asarray(L0_ffn_f1, np.float32).reshape(D, 4096))
    f2w = np.ascontiguousarray(np.asarray(L0_ffn_f2, np.float32).reshape(4096, D))
    biasT = np.ascontiguousarray(np.asarray(L0_ffn_bias, np.float32).reshape(64, 64).T)
    araw = np.ascontiguousarray(np.asarray(L0_attn_alpha_raw, np.float32).reshape(1, 16))
    braw = np.ascontiguousarray(np.asarray(L0_attn_beta_raw, np.float32).reshape(1, 16))
    ident, maskj, masklo, maskup, tie, gmask = _consts()

    nc, _ = build_nc()
    in_maps = []
    for c in range(NCORE):
        xc = np.ascontiguousarray(x[2 * c:2 * c + 2].reshape(TC, D))
        in_maps.append(dict(
            x=xc, wqT=wqT, wkT=wkT, wvT=wvT, woT=woT, ctrl=ctrl, f1w=f1w,
            f2w=f2w, biasT=biasT, araw=araw, braw=braw, ident=ident,
            maskj=maskj, masklo=masklo, maskup=maskup, tie=tie, gmask=gmask))
    res = run_bass_kernel_spmd(nc, in_maps, core_ids=list(range(NCORE)))
    out = np.empty((16, 2048, D), np.float32)
    for c in range(NCORE):
        out[2 * c:2 * c + 2] = np.asarray(res.results[c]["out"], np.float32).reshape(2, 2048, D)
    return out


# version 10 (final)
# speedup vs baseline: 1.0221x; 1.0047x over previous
import numpy as np
import ml_dtypes
from contextlib import ExitStack
import concourse.bass as bass
import concourse.mybir as mybir
from concourse.bass_utils import run_bass_kernel_spmd

F32 = mybir.dt.float32
BF16 = mybir.dt.bfloat16
AF = mybir.ActivationFunctionType
ALU = mybir.AluOpType
AX = mybir.AxisListType.X
bf16 = ml_dtypes.bfloat16

NCORE = 8
TC = 4096
D = 1024
NH = 16
DH = 64
CH = 64
NBAND = 8
BTB = 256
BT = 512
TPB = 4
CPB = 4

_DBG = {"upto": None, "outs": []}


class Em:
    ENG = dict(t="tensor", v="vector", s="scalar", y="sync", g="gpsimd")

    def __init__(self, nc, block, ctx):
        self.nc = nc
        self.block = block
        self.sem = {k: ctx.enter_context(nc.semaphore("sem_" + k)) for k in self.ENG}
        self.cnt = {k: 0 for k in self.ENG}
        self.lastw = {}

    def stage(self, ek, fns, waits=None, dma=False, selfdep=False, noinc=False):
        sem = self.sem
        lw = self.lastw
        wl = [(k, v) for k, v in (waits or {}).items() if k != ek and v > 0]
        own = self.cnt[ek]

        serial_dma = dma and ek == "g"
        def body(e):
            if (selfdep or dma) and own > 0 and lw.get((ek, ek), 0) < own:
                e.wait_ge(sem[ek], own)
                lw[(ek, ek)] = own
            for k, v in wl:
                if lw.get((ek, k), 0) >= v:
                    continue
                e.wait_ge(sem[k], v)
                lw[(ek, k)] = v
            run = own
            for i, f in enumerate(fns):
                if serial_dma and i > 0:
                    e.wait_ge(sem[ek], run)
                    lw[(ek, ek)] = run
                ins = f(e)
                if not noinc:
                    ins.then_inc(sem[ek], 16 if dma else 1)
                run += 16 if dma else 1

        getattr(self.block, self.ENG[ek])(body)
        if not noinc:
            self.cnt[ek] += (16 if dma else 1) * len(fns)
        return self.cnt[ek]

    def snap(self):
        return dict(self.cnt)

    def barrier(self):
        c = self.snap()
        for ek in self.ENG:
            w = {k: v for k, v in c.items() if k != ek}
            self.stage(ek, [lambda e: e.nop()], waits=w, noinc=True)


def build_nc(dbg_upto=None):
    nc = bass.Bass("TRN2", target_bir_lowering=False)
    ap = {}
    def din(name, shape, dt=F32):
        ap[name] = nc.dram_tensor(name, shape, dt, kind="ExternalInput").ap()
        return ap[name]

    x_h = din("x", [TC, D])
    wq_h = din("wqT", [D, D])
    wk_h = din("wkT", [D, D])
    wv_h = din("wvT", [D, D])
    wo_h = din("woT", [D, D])
    ctrl_h = din("ctrl", [D, 64])
    f1_h = din("f1w", [D, 4096])
    f2_h = din("f2w", [4096, D])
    biasT_h = din("biasT", [64, 64])
    araw_h = din("araw", [1, 16])
    braw_h = din("braw", [1, 16])
    ident_h = din("ident", [128, 128])
    maskj_h = din("maskj", [64, 64])
    masklo_h = din("masklo", [64, 64])
    maskup_h = din("maskup", [64, 64])
    tie_h = din("tie", [1, 128])
    gmask_h = din("gmask", [128, 8])

    out_h = nc.dram_tensor("out", [TC, D], F32, kind="ExternalOutput").ap()
    x1_d = nc.dram_tensor("x1d", [TC, D], BF16, kind="Internal").ap()
    inter_d = nc.dram_tensor("interd", [64 * 256, D], BF16, kind="Internal").ap()

    dbg_outs = []
    def dbg(name, shape, dt=BF16):
        a = nc.dram_tensor(name, shape, dt, kind="ExternalOutput").ap()
        dbg_outs.append(name)
        return a

    with ExitStack() as ctx:
        em_ctx = ctx
        ident_bf = ctx.enter_context(nc.sbuf_tensor("ident_bf", [128, 128], BF16))
        maskj_bf = ctx.enter_context(nc.sbuf_tensor("maskj_bf", [128, 64], BF16))
        mlo128 = ctx.enter_context(nc.sbuf_tensor("mlo128", [128, 64], F32))
        mup128 = ctx.enter_context(nc.sbuf_tensor("mup128", [128, 64], F32))
        abrow = ctx.enter_context(nc.sbuf_tensor("abrow", [1, 128], F32))
        abbc = ctx.enter_context(nc.sbuf_tensor("abbc", [128, 48], F32))
        ones1 = ctx.enter_context(nc.sbuf_tensor("ones1", [1, 128], F32))
        cst = ctx.enter_context(nc.sbuf_tensor("cst", [128, 4], F32))
        onescol = ctx.enter_context(nc.sbuf_tensor("onescol", [128, 1], BF16))
        tie_sb = ctx.enter_context(nc.sbuf_tensor("tie_sb", [1, 128], F32))
        ctrl_sb = ctx.enter_context(nc.sbuf_tensor("ctrl_sb", [128, 512], BF16))
        biasT_sb = ctx.enter_context(nc.sbuf_tensor("biasT_sb", [64, 64], F32))
        gmask_sb = ctx.enter_context(nc.sbuf_tensor("gmask_sb", [128, 8], BF16))

        block = ctx.enter_context(nc.Block())
        em = Em(nc, block, em_ctx)
        E = em.stage

        init_ctx = ExitStack()
        psIN = init_ctx.enter_context(nc.psum_tensor("psIN", [128, 512], F32))

        E("g", [
            lambda g: g.dma_start(out=ident_bf[:], in_=ident_h),
            lambda g: g.dma_start(out=maskj_bf[0:64, :], in_=maskj_h),
            lambda g: g.dma_start(out=maskj_bf[64:128, :], in_=maskj_h),
            lambda g: g.dma_start(out=mlo128[0:64, :], in_=masklo_h),
            lambda g: g.dma_start(out=mlo128[64:128, :], in_=masklo_h),
            lambda g: g.dma_start(out=mup128[0:64, :], in_=maskup_h),
            lambda g: g.dma_start(out=mup128[64:128, :], in_=maskup_h),
            lambda g: g.dma_start(out=abrow[:, 0:16], in_=araw_h),
            lambda g: g.dma_start(out=abrow[:, 16:32], in_=braw_h),
            lambda g: g.dma_start(out=tie_sb[:], in_=tie_h),
            lambda g: g.dma_start(
                out=ctrl_sb[:, :].rearrange("p (c n) -> p c n", c=8),
                in_=ctrl_h.rearrange("(c p) n -> p c n", p=128)),
            lambda g: g.dma_start(out=biasT_sb[:], in_=biasT_h),
            lambda g: g.dma_start(out=gmask_sb[:], in_=gmask_h),
        ], dma=True)
        g0 = em.cnt["g"]

        E("v", [lambda v: v.memset(ones1[:], 1.0),
                lambda v: v.memset(onescol[:], 1.0),
                lambda v: v.memset(cst[:, 0:1], 1e-5),
                lambda v: v.memset(cst[:, 1:2], 1e-24),
                lambda v: v.memset(cst[:, 3:4], 10.0)])
        E("v", [lambda v: v.reciprocal(cst[:, 2:3], cst[:, 3:4])], selfdep=True)
        E("s", [
            lambda s: s.activation(abrow[:, 32:48], abrow[:, 0:16], AF.Sigmoid),
            lambda s: s.activation(abrow[:, 48:64], abrow[:, 16:32], AF.Sigmoid),
        ], waits={"g": g0})
        E("v", [
            lambda v: v.tensor_scalar_mul(abrow[:, 64:80], abrow[:, 32:48], 0.5),
            lambda v: v.tensor_scalar_mul(abrow[:, 80:96], abrow[:, 48:64], 2.0),
        ], waits={"s": em.cnt["s"]})
        E("v", [
            lambda v: v.tensor_scalar_mul(abrow[:, 96:112], abrow[:, 64:80], -1.0),
            lambda v: v.reciprocal(abrow[:, 48:64], abrow[:, 80:96]),
        ], selfdep=True)
        E("v", [
            lambda v: v.tensor_tensor(abrow[:, 112:128], abrow[:, 96:112], abrow[:, 48:64], ALU.mult),
        ], selfdep=True)
        vrows = em.cnt["v"]
        E("t", [
            lambda t: t.matmul(psIN[:, 0:16], ones1[:], abrow[:, 80:96], start=True, stop=True),
            lambda t: t.matmul(psIN[:, 16:32], ones1[:], abrow[:, 112:128], start=True, stop=True),
            lambda t: t.matmul(psIN[:, 32:48], ones1[:], abrow[:, 96:112], start=True, stop=True),
        ], waits={"v": vrows})
        E("v", [lambda v: v.tensor_copy(abbc[:, 0:48], psIN[:, 0:48])], waits={"t": em.cnt["t"]})
        vbc = em.cnt["v"]
        init_ctx.close()

        with ExitStack() as pa:
            wq_sb = pa.enter_context(nc.sbuf_tensor("wq_sb", [128, 8192], BF16))
            wk_sb = pa.enter_context(nc.sbuf_tensor("wk_sb", [128, 8192], BF16))
            wv_sb = pa.enter_context(nc.sbuf_tensor("wv_sb", [128, 8192], BF16))
            wo_sb = pa.enter_context(nc.sbuf_tensor("wo_sb", [128, 8192], BF16))
            mA0 = pa.enter_context(nc.sbuf_tensor("mA0", [128, 1024], BF16))
            mB0 = pa.enter_context(nc.sbuf_tensor("mB0", [128, 1024], BF16))
            H_sb = pa.enter_context(nc.sbuf_tensor("H_sb", [128, 1024], BF16))
            M_sb = pa.enter_context(nc.sbuf_tensor("M_sb", [128, 1024], BF16))
            x_sb = pa.enter_context(nc.sbuf_tensor("x_sb", [128, 4096], F32))
            xr_sb = x_sb
            xn_tm = pa.enter_context(nc.sbuf_tensor("xn_tm", [128, 4096], BF16))
            xn_fm = pa.enter_context(nc.sbuf_tensor("xn_fm", [128, 4096], BF16))
            q_fm = pa.enter_context(nc.sbuf_tensor("q_fm", [128, 4096], BF16))
            k_fm = pa.enter_context(nc.sbuf_tensor("k_fm", [128, 4096], BF16))
            k_tm = pa.enter_context(nc.sbuf_tensor("k_tm", [128, 4096], BF16))
            kb_tm = pa.enter_context(nc.sbuf_tensor("kb_tm", [128, 4096], BF16))
            v_tm = pa.enter_context(nc.sbuf_tensor("v_tm", [128, 4096], BF16))
            o_tm = xn_tm
            o_fm = xn_fm
            x1_sb = pa.enter_context(nc.sbuf_tensor("x1_sb", [128, 4096], BF16))
            sq_sb = pa.enter_context(nc.sbuf_tensor("sq_sb", [128, 1024], BF16))
            st_sb = pa.enter_context(nc.sbuf_tensor("st_sb", [128, 64], F32))
            rk_sb = pa.enter_context(nc.sbuf_tensor("rk_sb", [128, 64], F32))
            A_bufs = [pa.enter_context(nc.sbuf_tensor(f"A{k}", [128, 1024], BF16)) for k in range(6)]
            B_bufs = [pa.enter_context(nc.sbuf_tensor(f"B{k}", [128, 1024], BF16)) for k in range(5)]
            W_fin = [pa.enter_context(nc.sbuf_tensor(f"Wf{b}", [128, 1024], BF16)) for b in range(2)]
            PT_sb = pa.enter_context(nc.sbuf_tensor("PT_sb", [128, 1024], BF16))
            RHS_sb = pa.enter_context(nc.sbuf_tensor("RHS_sb", [128, 1024], BF16))
            tmp_f = pa.enter_context(nc.sbuf_tensor("tmp_f", [128, 1024], BF16))
            re_sb = pa.enter_context(nc.sbuf_tensor("re_sb", [128, 16], F32))

            psA = pa.enter_context(nc.psum_tensor("psA", [128, 1024], F32))
            psB = pa.enter_context(nc.psum_tensor("psB", [128, 1024], F32))
            psC = pa.enter_context(nc.psum_tensor("psC", [128, 1024], F32))
            psD = pa.enter_context(nc.psum_tensor("psD", [128, 512], F32))
            psT = pa.enter_context(nc.psum_tensor("psT", [128, 1024], BF16))

            E("g", [
                lambda g: g.dma_start(out=wq_sb[:, :].rearrange("p (c n) -> p c n", c=8),
                                      in_=wq_h.rearrange("(c p) n -> p c n", p=128)),
                lambda g: g.dma_start(out=wk_sb[:, :].rearrange("p (c n) -> p c n", c=8),
                                      in_=wk_h.rearrange("(c p) n -> p c n", p=128)),
                lambda g: g.dma_start(out=wv_sb[:, :].rearrange("p (c n) -> p c n", c=8),
                                      in_=wv_h.rearrange("(c p) n -> p c n", p=128)),
                lambda g: g.dma_start(out=wo_sb[:, :].rearrange("p (c n) -> p c n", c=8),
                                      in_=wo_h.rearrange("(c p) n -> p c n", p=128)),
            ], dma=True)
            gW = em.cnt["g"]
            mask_ops = [lambda v: v.memset(H_sb[:], 0.0), lambda v: v.memset(M_sb[:], 0.0)]
            for h in range(NH):
                hr = (h % 2) * 8 + h // 2
                mask_ops.append(lambda v, h=h, hr=hr: v.tensor_scalar_mul(
                    mA0[:, hr * 64:(hr + 1) * 64], mlo128[:], abbc[:, 32 + h:33 + h]))
                mask_ops.append(lambda v, h=h, hr=hr: v.tensor_scalar_mul(
                    mB0[:, hr * 64:(hr + 1) * 64], mup128[:], abbc[:, 32 + h:33 + h]))
            E("v", mask_ops, waits={"g": g0}, selfdep=True)
            init_v = em.cnt["v"]

            for band in range(NBAND):
                r0 = band * BTB
                r1 = 2048 + band * BTB
                yx = E("y", [
                    lambda y: y.dma_start(
                        out=x_sb[:, 0:2048].rearrange("p (n m) -> p n m", n=2),
                        in_=x_h[r0:r0 + 256, :].rearrange("(n p) m -> p n m", p=128)),
                    lambda y: y.dma_start(
                        out=x_sb[:, 2048:4096].rearrange("p (n m) -> p n m", n=2),
                        in_=x_h[r1:r1 + 256, :].rearrange("(n p) m -> p n m", p=128)),
                ], dma=True, waits={"v": em.cnt["v"], "t": em.cnt["t"], "s": em.cnt["s"]})

                for tl in range(TPB):
                    xt = x_sb[:, tl * 1024:(tl + 1) * 1024]
                    sS = E("s", [lambda s, xt=xt: s.activation(sq_sb[:], xt, AF.Square)],
                           waits={"y": yx, "v": em.cnt["v"]})
                    E("v", [
                        lambda v, xt=xt, tl=tl: v.reduce_sum(st_sb[:, tl * 8:tl * 8 + 1].unsqueeze(2),
                                                             xt.unsqueeze(1), axis=AX),
                    ], waits={"y": yx})
                    E("v", [
                        lambda v, tl=tl: v.reduce_sum(st_sb[:, tl * 8 + 1:tl * 8 + 2].unsqueeze(2),
                                                      sq_sb[:, :].unsqueeze(1), axis=AX),
                    ], waits={"s": sS})
                vred = em.cnt["v"]
                stm = st_sb[:, 32:36]; ste = st_sb[:, 36:40]; stv = st_sb[:, 40:44]
                str_ = st_sb[:, 44:48]; stmr = st_sb[:, 48:52]
                s1 = st_sb[:, 0:32].rearrange("p (t k) -> p t k", k=8)[:, :, 0]
                s2 = st_sb[:, 0:32].rearrange("p (t k) -> p t k", k=8)[:, :, 1]
                E("v", [
                    lambda v: v.tensor_scalar_mul(stm, s1, 1.0 / D),
                    lambda v: v.tensor_scalar_mul(ste, s2, 1.0 / D),
                ], selfdep=True)
                E("v", [lambda v: v.tensor_tensor(stv, stm, stm, ALU.mult)], selfdep=True)
                E("v", [lambda v: v.tensor_sub(stv, ste, stv)], selfdep=True)
                E("s", [lambda s: s.activation(str_, stv, AF.Ln, bias=cst[:, 0:1])],
                  waits={"v": em.cnt["v"]})
                sLN = E("s", [lambda s: s.activation(str_, str_, AF.Exp, scale=-0.5)],
                        selfdep=True)
                E("v", [lambda v: v.tensor_tensor(stmr, stm, str_, ALU.mult)],
                  waits={"s": sLN}, selfdep=True)
                for tl in range(TPB):
                    E("v", [lambda v, tl=tl: v.tensor_scalar(
                        xn_tm[:, tl * 1024:(tl + 1) * 1024], x_sb[:, tl * 1024:(tl + 1) * 1024],
                        str_[:, tl:tl + 1], stmr[:, tl:tl + 1], ALU.mult, ALU.subtract)],
                      waits={"t": em.cnt["t"]}, selfdep=(tl == 0))
                vxn = em.cnt["v"]

                tT0 = em.cnt["t"]
                for kc in range(8):
                    tt = E("t", [
                        lambda t, kc=kc, tl2=tl2: t.transpose(
                            psT[:, tl2 * 128:(tl2 + 1) * 128],
                            xn_tm[:, tl2 * 1024 + kc * 128:tl2 * 1024 + kc * 128 + 128],
                            ident_bf[:])
                        for tl2 in range(TPB)
                    ], waits={"v": vxn} if kc == 0 else {"v": em.cnt["v"]})
                    E("v", [lambda v, kc=kc: v.tensor_copy(
                        xn_fm[:, kc * 512:(kc + 1) * 512], psT[:, 0:512])],
                      waits={"t": tt})
                vxnf = em.cnt["v"]

                for oc in range(8):
                    tq = E("t", [
                        lambda t, oc=oc, kc2=kc2: t.matmul(
                            psA[:, 0:512],
                            wq_sb[:, kc2 * 1024 + oc * 128:kc2 * 1024 + oc * 128 + 128],
                            xn_fm[:, kc2 * 512:(kc2 + 1) * 512],
                            start=(kc2 == 0), stop=(kc2 == 7))
                        for kc2 in range(8)
                    ], waits=({"v": vxnf, "g": gW} if oc == 0 else {"v": em.cnt["v"]}))
                    E("v", [lambda v, oc=oc: v.tensor_copy(
                        q_fm[:, oc * 512:(oc + 1) * 512], psA[:, 0:512])], waits={"t": tq})
                vq = em.cnt["v"]

                for tl in range(TPB):
                    tkv = E("t", [
                        (lambda t, tl=tl, kc2=kc2, which=w, half=hf: t.matmul(
                            (psB if which == 0 else psC)[:, half * 512:half * 512 + 512],
                            xn_fm[:, kc2 * 512 + tl * 128:kc2 * 512 + tl * 128 + 128],
                            (wk_sb if which == 0 else wv_sb)[:, kc2 * 1024 + half * 512:kc2 * 1024 + half * 512 + 512],
                            start=(kc2 == 0), stop=(kc2 == 7)))
                        for kc2 in range(8) for w in (0, 1) for hf in (0, 1)
                    ], waits={"v": em.cnt["v"], "g": gW, "s": em.cnt["s"]})
                    E("s", [lambda s: s.activation(sq_sb[:], psB[:, 0:1024], AF.Square)],
                      waits={"t": tkv})
                    E("v", [lambda v, tl=tl: v.reduce_sum(
                        rk_sb[:, 32:48].unsqueeze(2),
                        sq_sb[:, :].rearrange("p (a b) -> p a b", b=64), axis=AX)],
                      waits={"s": em.cnt["s"]})
                    E("v", [lambda v: v.tensor_scalar_max(rk_sb[:, 32:48], rk_sb[:, 32:48], cst[:, 1:2])],
                      selfdep=True)
                    E("s", [lambda s: s.activation(rk_sb[:, 0:16], rk_sb[:, 32:48], AF.Ln)],
                      waits={"v": em.cnt["v"]})
                    sRK = E("s", [lambda s: s.activation(rk_sb[:, 0:16], rk_sb[:, 0:16], AF.Exp, scale=-0.5)],
                            selfdep=True)
                    E("v", [lambda v: v.tensor_tensor(rk_sb[:, 16:32], rk_sb[:, 0:16], abbc[:, 0:16], ALU.mult)],
                      waits={"s": sRK})
                    _rkb = em.cnt["v"]
                    E("v", [
                        lambda v, tl=tl: v.tensor_tensor(
                            k_tm[:, tl * 1024:(tl + 1) * 1024].rearrange("p (a b) -> p a b", b=64),
                            psB[:, 0:1024].rearrange("p (a b) -> p a b", b=64),
                            rk_sb[:, 0:16].unsqueeze(2).broadcast_to([128, 16, 64]),
                            ALU.mult),
                        lambda v, tl=tl: v.tensor_tensor(
                            kb_tm[:, tl * 1024:(tl + 1) * 1024].rearrange("p (a b) -> p a b", b=64),
                            psB[:, 0:1024].rearrange("p (a b) -> p a b", b=64),
                            rk_sb[:, 16:32].unsqueeze(2).broadcast_to([128, 16, 64]),
                            ALU.mult),
                    ], selfdep=True)
                    E("s", [lambda s, tl=tl: s.activation(
                        v_tm[:, tl * 1024:(tl + 1) * 1024], psC[:, 0:1024], AF.Copy)])
                vkv = em.cnt["v"]
                skv = em.cnt["s"]

                for oc in range(8):
                    tt = E("t", [
                        lambda t, oc=oc, tl2=tl2: t.transpose(
                            psT[:, tl2 * 128:(tl2 + 1) * 128],
                            k_tm[:, tl2 * 1024 + oc * 128:tl2 * 1024 + oc * 128 + 128],
                            ident_bf[:])
                        for tl2 in range(TPB)
                    ], waits={"v": vkv if oc == 0 else em.cnt["v"], "s": skv})
                    E("v", [lambda v, oc=oc: v.tensor_copy(
                        k_fm[:, oc * 512:(oc + 1) * 512], psT[:, 0:512])], waits={"t": tt})
                vkf = em.cnt["v"]

                if dbg_upto == "qkv" and band == 0:
                    _emit_dbg_qkv(nc, em, E, locals(), dbg)
                    break

                for cp in range(2):
                    for b in range(2):
                        gfns = []
                        for h in range(NH):
                            hb = (h % 2) * 64
                            hr = (h % 2) * 8 + h // 2
                            col = (h // 2) * 512 + b * 256
                            for ci in range(2):
                                c = 2 * cp + ci
                                ksl = k_fm[hb:hb + 64, col + c * 64:col + c * 64 + 64]
                                gfns.append(lambda t, ksl=ksl, hr=hr, h=h, ci=ci: t.matmul(
                                    psA[ci * 64:ci * 64 + 64, hr * 64:hr * 64 + 64],
                                    ksl, ksl, start=True, stop=True,
                                    tile_position=((h % 2) * 64, ci * 64)))
                        tG = E("t", gfns, waits={"v": em.cnt["v"], "s": em.cnt["s"]})
                        vM = E("v", [
                            lambda v: v.tensor_tensor(A_bufs[0][:], psA[:, 0:1024], mA0[:], ALU.mult),
                            lambda v: v.tensor_tensor(B_bufs[0][:], psA[:, 0:1024], mB0[:], ALU.mult),
                        ], waits={"t": tG, "v": init_v})
                        if dbg_upto == "gmask":
                            break
                        for k in range(5):
                            sq_fns = []
                            for h in range(NH):
                                for par in (0, 64):
                                    sq_fns.append(lambda t, k=k, h=h, par=par: t.matmul(
                                        psB[par:par + 64, h * 64:h * 64 + 64],
                                        B_bufs[k][par:par + 64, h * 64:h * 64 + 64],
                                        A_bufs[k][par:par + 64, h * 64:h * 64 + 64],
                                        start=True, stop=True, tile_position=(par, par)))
                                    if k < 4:
                                        sq_fns.append(lambda t, k=k, h=h, par=par: t.matmul(
                                            psC[par:par + 64, h * 64:h * 64 + 64],
                                            A_bufs[k][par:par + 64, h * 64:h * 64 + 64],
                                            B_bufs[k][par:par + 64, h * 64:h * 64 + 64],
                                            start=True, stop=True, tile_position=(par, par)))
                            tS = E("t", sq_fns, waits={"v": vM, "s": em.cnt["s"]})
                            if k < 4:
                                E("s", [lambda s, k=k: s.copy(B_bufs[k + 1][:], psC[:, 0:1024])],
                                  waits={"t": tS})
                            vM = E("v", [lambda v, k=k: v.tensor_copy(A_bufs[k + 1][:], psB[:, 0:1024])],
                                   waits={"t": tS})
                        kb_cols = [None, None]
                        for ci in range(2):
                            c = 2 * cp + ci
                            tl = b * 2 + c // 2
                            kb_cols[ci] = tl * 1024
                        tlp = b * 2 + cp
                        kbt = kb_tm[:, tlp * 1024:(tlp + 1) * 1024]
                        Wcur = None
                        for k in range(6):
                            ap_fns = []
                            for h in range(NH):
                                hr = (h % 2) * 8 + h // 2
                                rsl = (kbt[:, h * 64:h * 64 + 64] if k == 0
                                       else Wcur[:, hr * 64:hr * 64 + 64])
                                for par in (0, 64):
                                    ap_fns.append(lambda t, k=k, hr=hr, par=par, rsl=rsl: t.matmul(
                                        psB[par:par + 64, hr * 64:hr * 64 + 64],
                                        A_bufs[k][par:par + 64, hr * 64:hr * 64 + 64],
                                        rsl[par:par + 64, :],
                                        start=True, stop=True, tile_position=(par, par)))
                            tA = E("t", ap_fns, waits={"v": vM, "s": em.cnt["s"]})
                            Wnext = W_fin[b] if k == 5 else A_bufs[k]
                            if k == 0:
                                vM = E("v", [lambda v, Wnext=Wnext, kbt=kbt: v.tensor_tensor(
                                    Wnext[:, :].rearrange("p (b a c) -> p b a c", b=2, a=8),
                                    psB[:, 0:1024].rearrange("p (b a c) -> p b a c", b=2, a=8),
                                    kbt.rearrange("p (a b c) -> p b a c", a=8, b=2),
                                    ALU.add)], waits={"t": tA}, selfdep=True)
                            else:
                                vM = E("v", [lambda v, Wnext=Wnext, Wcur=Wcur: v.tensor_tensor(
                                    Wnext[:], psB[:, 0:1024], Wcur[:], ALU.add)],
                                    waits={"t": tA}, selfdep=True)
                            Wcur = Wnext
                        if dbg_upto == "solve" and cp == 0 and b == 0:
                            break
                    vSolve = em.cnt["v"]
                    if dbg_upto in ("solve", "gmask") and cp == 0:
                        break

                    for ci in range(2):
                        c = 2 * cp + ci
                        cb = ci * 64
                        pbase = (c % 2) * 64
                        for b in range(2):
                            tl = b * 2 + c // 2
                            fns = []
                            for h in range(NH):
                                hb = (h % 2) * 64
                                hr = (h % 2) * 8 + h // 2
                                fcol = (h // 2) * 512 + b * 256 + c * 64
                                ksl = k_fm[hb:hb + 64, fcol:fcol + 64]
                                qsl = q_fm[hb:hb + 64, fcol:fcol + 64]
                                fns.append(lambda t, ksl=ksl, qsl=qsl, hr=hr, h=h, pbase=pbase: t.matmul(
                                    psA[pbase:pbase + 64, hr * 64:hr * 64 + 64], ksl, qsl,
                                    start=True, stop=True, tile_position=((h % 2) * 64, pbase)))
                                fns.append(lambda t, ksl=ksl, hr=hr, h=h, b=b, pbase=pbase: t.matmul(
                                    psB[pbase:pbase + 64, hr * 64:hr * 64 + 64], ksl,
                                    H_sb[(h % 2) * 64:(h % 2) * 64 + 64, b * 512 + (h // 2) * 64:b * 512 + (h // 2) * 64 + 64],
                                    start=True, stop=True, tile_position=((h % 2) * 64, pbase)))
                                fns.append(lambda t, qsl=qsl, hr=hr, h=h, b=b, pbase=pbase: t.matmul(
                                    psC[pbase:pbase + 64, hr * 64:hr * 64 + 64], qsl,
                                    M_sb[(h % 2) * 64:(h % 2) * 64 + 64, b * 512 + (h // 2) * 64:b * 512 + (h // 2) * 64 + 64],
                                    start=True, stop=True, tile_position=((h % 2) * 64, pbase)))
                            tSMm = E("t", fns, waits={"v": em.cnt["v"]})
                            sE = E("s", [lambda s, pbase=pbase: s.activation(
                                PT_sb[pbase:pbase + 64, :], psA[pbase:pbase + 64, 0:1024],
                                AF.Exp, scale=0.125)], waits={"t": tSMm})
                            vP = E("v", [
                                lambda v, pbase=pbase: v.tensor_tensor(
                                    PT_sb[pbase:pbase + 64, :].rearrange("p (a b) -> p a b", b=64),
                                    PT_sb[pbase:pbase + 64, :].rearrange("p (a b) -> p a b", b=64),
                                    maskj_bf[pbase:pbase + 64, :].unsqueeze(1).broadcast_to([64, 16, 64]),
                                    ALU.mult),
                                lambda v, pbase=pbase: v.tensor_tensor(
                                    tmp_f[pbase:pbase + 64, :].rearrange("p (b a c) -> p b a c", b=2, a=8),
                                    psB[pbase:pbase + 64, 0:1024].rearrange("p (b a c) -> p b a c", b=2, a=8),
                                    abbc[pbase:pbase + 64, 16:32].rearrange("p (a b) -> p b a", a=8)
                                        .unsqueeze(3).broadcast_to([64, 2, 8, 64]),
                                    ALU.mult),
                            ], waits={"s": sE, "t": tSMm})
                            vR = E("v", [
                                lambda v, tl=tl, pbase=pbase: v.tensor_add(
                                    RHS_sb[pbase:pbase + 64, :].rearrange("p (b a c) -> p b a c", b=2, a=8),
                                    tmp_f[pbase:pbase + 64, :].rearrange("p (b a c) -> p b a c", b=2, a=8),
                                    v_tm[pbase:pbase + 64, tl * 1024:(tl + 1) * 1024]
                                        .rearrange("p (a b c) -> p b a c", a=8, b=2)),
                            ], selfdep=True)
                            fns = []
                            for h in range(NH):
                                hr = (h % 2) * 8 + h // 2
                                psl = PT_sb[pbase:pbase + 64, hr * 64:hr * 64 + 64]
                                fns.append(lambda t, psl=psl, hr=hr, pbase=pbase: t.matmul(
                                    psD[pbase:pbase + 64, hr:hr + 1], psl, onescol[pbase:pbase + 64, :],
                                    start=True, stop=True, tile_position=(pbase, pbase)))
                                fns.append(lambda t, psl=psl, hr=hr, h=h, tl=tl, pbase=pbase: t.matmul(
                                    psA[pbase:pbase + 64, hr * 64:hr * 64 + 64], psl,
                                    v_tm[pbase:pbase + 64, tl * 1024 + h * 64:tl * 1024 + h * 64 + 64],
                                    start=True, stop=True, tile_position=(pbase, pbase)))
                            tPV = E("t", fns, waits={"v": vP, "s": sE})
                            fns = []
                            for h in range(NH):
                                hr = (h % 2) * 8 + h // 2
                                wsl = W_fin[b][pbase:pbase + 64, hr * 64:hr * 64 + 64]
                                rsl = RHS_sb[pbase:pbase + 64, hr * 64:hr * 64 + 64]
                                fns.append(lambda t, wsl=wsl, rsl=rsl, h=h, pbase=pbase: t.matmul(
                                    psB[(h % 2) * 64:(h % 2) * 64 + 64, (h // 2) * 64:(h // 2) * 64 + 64],
                                    wsl, rsl, start=True, stop=True, tile_position=(pbase, (h % 2) * 64)))
                                fns.append(lambda t, wsl=wsl, rsl=rsl, h=h, pbase=pbase: t.matmul(
                                    psB[(h % 2) * 64:(h % 2) * 64 + 64, 512 + (h // 2) * 64:512 + (h // 2) * 64 + 64],
                                    rsl, wsl, start=True, stop=True, tile_position=(pbase, (h % 2) * 64)))
                            tHM = E("t", fns, waits={"v": vR})
                            vRe = E("v", [lambda v, pbase=pbase: v.reciprocal(
                                re_sb[pbase:pbase + 64, 0:16], psD[pbase:pbase + 64, 0:16])],
                                waits={"t": tPV})
                            E("v", [
                                lambda v, pbase=pbase: v.tensor_tensor(
                                    tmp_f[pbase:pbase + 64, :].rearrange("p (a b) -> p a b", b=64),
                                    psA[pbase:pbase + 64, 0:1024].rearrange("p (a b) -> p a b", b=64),
                                    re_sb[pbase:pbase + 64, 0:16].unsqueeze(2).broadcast_to([64, 16, 64]),
                                    ALU.mult),
                            ], selfdep=True)
                            E("v", [
                                lambda v, tl=tl, pbase=pbase: v.tensor_add(
                                    o_tm[pbase:pbase + 64, tl * 1024:(tl + 1) * 1024]
                                        .rearrange("p (a b c) -> p b a c", a=8, b=2),
                                    tmp_f[pbase:pbase + 64, :]
                                        .rearrange("p (b a c) -> p b a c", b=2, a=8),
                                    psC[pbase:pbase + 64, 0:1024]
                                        .rearrange("p (b a c) -> p b a c", b=2, a=8)),
                            ], selfdep=True)
                            E("v", [
                                lambda v, b=b: v.tensor_add(
                                    H_sb[:, b * 512:(b + 1) * 512],
                                    H_sb[:, b * 512:(b + 1) * 512],
                                    psB[:, 0:512]),
                                lambda v, b=b: v.scalar_tensor_tensor(
                                    M_sb[:, b * 512:(b + 1) * 512],
                                    psB[:, 512:1024], cst[:, 2:3],
                                    M_sb[:, b * 512:(b + 1) * 512],
                                    ALU.mult, ALU.add),
                            ], waits={"t": tHM})

                if dbg_upto in ("solve", "gmask") and band == 0:
                    _emit_dbg_solve(nc, em, E, locals(), dbg)
                    break
                if dbg_upto == "chain" and band == 0:
                    _emit_dbg_chain(nc, em, E, locals(), dbg)
                    break

                vO = em.cnt["v"]
                for oc in range(8):
                    tt = E("t", [
                        lambda t, oc=oc, tl2=tl2: t.transpose(
                            psT[:, tl2 * 128:(tl2 + 1) * 128],
                            o_tm[:, tl2 * 1024 + oc * 128:tl2 * 1024 + oc * 128 + 128],
                            ident_bf[:])
                        for tl2 in range(TPB)
                    ], waits={"v": vO if oc == 0 else em.cnt["v"]})
                    E("v", [lambda v, oc=oc: v.tensor_copy(
                        o_fm[:, oc * 512:(oc + 1) * 512], psT[:, 0:512])], waits={"t": tt})
                vOf = em.cnt["v"]
                yxr = em.cnt["y"]
                for tl in range(TPB):
                    tW = E("t", [
                        (lambda t, tl=tl, oc2=oc2, half=hf: t.matmul(
                            psA[:, half * 512:half * 512 + 512],
                            o_fm[:, oc2 * 512 + tl * 128:oc2 * 512 + tl * 128 + 128],
                            wo_sb[:, oc2 * 1024 + half * 512:oc2 * 1024 + half * 512 + 512],
                            start=(oc2 == 0), stop=(oc2 == 7)))
                        for oc2 in range(8) for hf in (0, 1)
                    ], waits={"v": em.cnt["v"]})
                    E("v", [lambda v, tl=tl: v.tensor_add(
                        x1_sb[:, tl * 1024:(tl + 1) * 1024],
                        xr_sb[:, tl * 1024:(tl + 1) * 1024],
                        psA[:, 0:1024])], waits={"t": tW})
                vx1 = em.cnt["v"]
                E("y", [
                    lambda y: y.dma_start(
                        out=x1_d[r0:r0 + 256, :].rearrange("(n p) m -> p n m", p=128),
                        in_=x1_sb[:, 0:2048].rearrange("p (n m) -> p n m", n=2)),
                    lambda y: y.dma_start(
                        out=x1_d[r1:r1 + 256, :].rearrange("(n p) m -> p n m", p=128),
                        in_=x1_sb[:, 2048:4096].rearrange("p (n m) -> p n m", n=2)),
                ], dma=True, waits={"v": vx1})
                if dbg_upto == "band1":
                    break

            em.barrier()

        if dbg_upto in ("qkv", "chain", "solve", "gmask"):
            _finalize_wait(em, E)
            return nc, dbg_outs
        if dbg_upto in ("phasea", "band1"):
            a = nc.dram_tensor("dbg_x1", [TC, D], BF16, kind="ExternalOutput").ap()
            dbg_outs.append("dbg_x1")
            E("y", [lambda y: y.dma_start(out=a, in_=x1_d)], dma=True, waits=em.snap())
            _finalize_wait(em, E)
            return nc, dbg_outs

        with ExitStack() as pb:
            sel_sb = pb.enter_context(nc.sbuf_tensor("sel_sb", [128, 65536], BF16))
            pselT_sb = pb.enter_context(nc.sbuf_tensor("pselT_sb", [128, 8192], BF16))
            psel_sb = pb.enter_context(nc.sbuf_tensor("psel_sb", [128, 512], BF16))
            x1t_sb = pb.enter_context(nc.sbuf_tensor("x1t_sb", [128, 1024], BF16))
            y_sb = pb.enter_context(nc.sbuf_tensor("y_sb", [128, 1024], BF16))
            yfbuf = pb.enter_context(nc.sbuf_tensor("yfbuf", [128, 1024], BF16))
            yf_sb = yfbuf[:, 0:1024]
            sqb_sb = yf_sb
            outf_sb = pb.enter_context(nc.sbuf_tensor("outf_sb", [128, 1024], F32))[:, :]
            stb_sb = pb.enter_context(nc.sbuf_tensor("stb_sb", [128, 16], F32))
            perm_sb = pb.enter_context(nc.sbuf_tensor("perm_sb", [64, 128], BF16))
            permT_sb = pb.enter_context(nc.sbuf_tensor("permT_sb", [128, 64], BF16))
            mx_sb = pb.enter_context(nc.sbuf_tensor("mx_sb", [64, 8], F32))
            f1a = pb.enter_context(nc.sbuf_tensor("f1a", [128, 4096], BF16))
            f1_sb = [f1a[:, 0:2048], pb.enter_context(nc.sbuf_tensor("f1_1", [128, 2048], BF16))[:, :]]
            f2_sb = [pb.enter_context(nc.sbuf_tensor(f"f2_{i}", [64, 4096], BF16)) for i in range(2)]
            inner_sb2 = [pb.enter_context(nc.sbuf_tensor(f"inner_{i}", [64, 128], BF16)) for i in range(2)]
            scat_sb = f1a
            inter_sb = pb.enter_context(nc.sbuf_tensor("inter_sb", [128, 4096], BF16))

            pb0 = pb.enter_context(nc.psum_tensor("pb0", [128, 512], F32))
            pb1 = pb.enter_context(nc.psum_tensor("pb1", [128, 512], F32))
            pb2 = pb.enter_context(nc.psum_tensor("pb2", [128, 512], F32))
            pb3 = pb.enter_context(nc.psum_tensor("pb3", [128, 1024], F32))
            pb4 = pb.enter_context(nc.psum_tensor("pb4", [128, 1024], F32))
            pbT = pb.enter_context(nc.psum_tensor("pbT", [128, 1024], BF16))

            for half in range(2):
                for tl in range(16):
                    gt = half * 16 + tl
                    row = gt * 128
                    yx1 = E("y", [lambda y, row=row: y.dma_start(
                        out=x1t_sb[:], in_=x1_d[row:row + 128, :])],
                        dma=True, waits={"t": em.cnt["t"], "v": em.cnt["v"], "s": em.cnt["s"]})
                    sS = E("s", [lambda s: s.activation(sqb_sb, x1t_sb[:], AF.Square)],
                           waits={"y": yx1})
                    E("v", [lambda v: v.reduce_sum(stb_sb[:, 0:1].unsqueeze(2),
                                                   x1t_sb[:, :].unsqueeze(1), axis=AX)],
                      waits={"y": yx1})
                    E("v", [lambda v: v.reduce_sum(stb_sb[:, 1:2].unsqueeze(2),
                                                   sqb_sb.unsqueeze(1), axis=AX)],
                      waits={"s": sS})
                    E("v", [
                        lambda v: v.tensor_scalar_mul(stb_sb[:, 2:3], stb_sb[:, 0:1], 1.0 / D),
                        lambda v: v.tensor_scalar_mul(stb_sb[:, 3:4], stb_sb[:, 1:2], 1.0 / D),
                    ], selfdep=True)
                    E("v", [lambda v: v.tensor_tensor(stb_sb[:, 4:5], stb_sb[:, 2:3], stb_sb[:, 2:3], ALU.mult)], selfdep=True)
                    E("v", [lambda v: v.tensor_sub(stb_sb[:, 4:5], stb_sb[:, 3:4], stb_sb[:, 4:5])], selfdep=True)
                    E("s", [lambda s: s.activation(stb_sb[:, 5:6], stb_sb[:, 4:5], AF.Ln, bias=cst[:, 0:1])],
                      waits={"v": em.cnt["v"]})
                    sLN = E("s", [lambda s: s.activation(stb_sb[:, 5:6], stb_sb[:, 5:6], AF.Exp, scale=-0.5)],
                            selfdep=True)
                    E("v", [lambda v: v.tensor_tensor(stb_sb[:, 6:7], stb_sb[:, 2:3], stb_sb[:, 5:6], ALU.mult)],
                      waits={"s": sLN}, selfdep=True)
                    vy = E("v", [lambda v: v.tensor_scalar(
                        y_sb[:], x1t_sb[:], stb_sb[:, 5:6], stb_sb[:, 6:7], ALU.mult, ALU.subtract)],
                        selfdep=True)
                    tT = E("t", [
                        lambda t, kc=kc: t.transpose(
                            pbT[:, kc * 128:(kc + 1) * 128],
                            y_sb[:, kc * 128:(kc + 1) * 128], ident_bf[:])
                        for kc in range(8)
                    ], waits={"v": vy})
                    vyf = E("v", [lambda v: v.tensor_copy(yf_sb, pbT[:, 0:1024])], waits={"t": tT})
                    tL = E("t", [
                        lambda t, kc=kc: t.matmul(
                            pb0[0:64, 0:128],
                            ctrl_sb[:, kc * 64:(kc + 1) * 64],
                            yf_sb[:, kc * 128:(kc + 1) * 128],
                            start=(kc == 0), stop=False)
                        for kc in range(8)
                    ] + [lambda t: t.matmul(
                        pb0[0:64, 0:128], ones1[:, 0:64], tie_sb[:, 0:128],
                        start=False, stop=True)], waits={"v": vyf})
                    vmx = E("v", [lambda v: v.reduce_max(
                        mx_sb[:, :].unsqueeze(2),
                        pb0[0:64, 0:128].rearrange("p (a b) -> p a b", b=16), axis=AX)],
                        waits={"t": tL})
                    vpm = E("v", [lambda v: v.tensor_tensor(
                        perm_sb[:, :].rearrange("p (a b) -> p a b", b=16),
                        pb0[0:64, 0:128].rearrange("p (a b) -> p a b", b=16),
                        mx_sb[:, :].unsqueeze(2).broadcast_to([64, 8, 16]),
                        ALU.is_equal)], selfdep=True)
                    tPT = E("t", [lambda t: t.transpose(
                        pbT[:, 0:64], perm_sb[:], ident_bf[0:64, 0:64])], waits={"v": vpm})
                    E("v", [lambda v: v.tensor_copy(permT_sb[:], pbT[:, 0:64])],
                      waits={"t": tPT}, selfdep=True)
                    vPS = E("v", [lambda v: v.tensor_tensor(
                        psel_sb[:, :].rearrange("p (e g) -> p e g", g=8),
                        permT_sb[:, :].unsqueeze(2).broadcast_to([128, 64, 8]),
                        gmask_sb[:, :].unsqueeze(1).broadcast_to([128, 64, 8]),
                        ALU.mult)], selfdep=True)
                    tPs = E("t", [
                        lambda t, k=k: t.transpose(
                            pbT[:, k * 128:(k + 1) * 128],
                            psel_sb[:, k * 128:(k + 1) * 128], ident_bf[:])
                        for k in range(4)
                    ], waits={"v": vPS})
                    E("v", [lambda v, tl=tl: v.tensor_copy(
                        pselT_sb[:, tl * 512:(tl + 1) * 512], pbT[:, 0:512])], waits={"t": tPs})
                    for kc in range(8):
                        pg = pb1 if kc % 2 == 0 else pb2
                        tg = E("t", [lambda t, kc=kc, pg=pg: t.matmul(
                            pg[:, 0:512], y_sb[:, kc * 128:(kc + 1) * 128], psel_sb[:],
                            start=True, stop=True)],
                            waits={"v": em.cnt["v"], "s": em.cnt["s"]})
                        ek2 = "v" if kc % 2 == 0 else "s"
                        if kc % 2 == 0:
                            E("v", [lambda v, kc=kc, tl=tl, pg=pg: v.tensor_copy(
                                sel_sb[:, (kc * 16 + tl) * 512:(kc * 16 + tl) * 512 + 512],
                                pg[:, 0:512])], waits={"t": tg})
                        else:
                            E("s", [lambda s, kc=kc, tl=tl, pg=pg: s.copy(
                                sel_sb[:, (kc * 16 + tl) * 512:(kc * 16 + tl) * 512 + 512],
                                pg[:, 0:512])], waits={"t": tg})
                vGather = em.cnt["v"]

                def f1_load(g, j, buf):
                    return g.dma_start(
                        out=f1_sb[buf].rearrange("p (c F) -> p c F", c=8),
                        in_=f1_h[:, j * 256:(j + 1) * 256].rearrange("(c p) F -> p c F", p=128))

                def f2_load(g, j, buf):
                    return g.dma_start(
                        out=f2_sb[buf][:, :].rearrange("p (el m) -> p el m", el=4),
                        in_=f2_h[j * 256:(j + 1) * 256, :].rearrange("(el p) m -> p el m", p=64))

                gF = E("g", [lambda g: f1_load(g, 0, 0), lambda g: f2_load(g, 0, 0)],
                       dma=True, waits={"t": em.cnt["t"]})
                relu_cnt = {0: em.cnt["s"], 1: em.cnt["s"]}
                inter_cnt = {0: em.cnt["v"], 1: em.cnt["v"]}
                for j in range(16):
                    buf = j % 2
                    if j < 15:
                        gF = E("g", [lambda g, j=j, buf=buf: f1_load(g, j + 1, 1 - buf),
                                     lambda g, j=j, buf=buf: f2_load(g, j + 1, 1 - buf)],
                               dma=True, waits={"t": em.cnt["t"]})
                        gw = gF - 32
                    else:
                        gw = gF
                    for el in range(4):
                        es = j * 4 + el
                        par = es % 2
                        pi = pb1 if par == 0 else pb2
                        po5 = pb3 if par == 0 else pb4
                        tF1 = E("t", [
                            lambda t, kc=kc, el=el, buf=buf, pi=pi: t.matmul(
                                pi[0:64, 0:128],
                                f1_sb[buf][:, kc * 256 + el * 64:kc * 256 + el * 64 + 64],
                                sel_sb[:, :].rearrange("p (c l g) -> p c l g", c=8, l=16)[:, kc, :, :]
                                      .rearrange("p l (e g) -> p l e g", g=8)[:, :, es, :],
                                start=(kc == 0), stop=(kc == 7))
                            for kc in range(8)
                        ], waits={"v": vGather, "g": gw,
                                  "s": relu_cnt.get(par, 0)})
                        sR = E("s", [lambda s, es=es, pi=pi, par=par: s.activation(
                            inner_sb2[par][:], pi[0:64, 0:128], AF.Relu,
                            bias=biasT_sb[:, es:es + 1])],
                            waits={"t": tF1})
                        relu_cnt[par] = sR
                        tF2 = E("t", [
                            lambda t, hf=hf, el=el, buf=buf, po5=po5, par=par: t.matmul(
                                po5[:, hf * 512:hf * 512 + 512], inner_sb2[par][:],
                                f2_sb[buf][:, el * 1024 + hf * 512:el * 1024 + hf * 512 + 512],
                                start=True, stop=True)
                            for hf in (0, 1)
                        ], waits={"s": sR, "v": inter_cnt.get(par, 0)})
                        E("v", [lambda v, el=el, po5=po5: v.tensor_copy(
                            inter_sb[:, el * 1024:(el + 1) * 1024], po5[:, 0:1024])],
                          waits={"t": tF2, "y": em.cnt["y"]})
                        inter_cnt[par] = em.cnt["v"]
                    vI8 = em.cnt["v"]
                    e0 = (j % 4) * 4
                    kblk = j // 4
                    E("y", [lambda y, e0=e0, kblk=kblk, half=half: y.dma_start(
                        out=inter_d.rearrange("(e R k) m -> e R k m", e=16, R=256)[
                            e0:e0 + 4, half * 128:half * 128 + 128, kblk, :].transpose([1, 0, 2]),
                        in_=inter_sb[:, :].rearrange("p (el m) -> p el m", el=4))],
                      dma=True, waits={"v": vI8})
                yInter = em.cnt["y"]

                for tl in range(16):
                    gt = half * 16 + tl
                    row = gt * 128
                    ysc = E("y", [
                        lambda y, tl=tl, half=half: y.dma_start(
                            out=scat_sb[:, :],
                            in_=inter_d.rearrange("(e R k) m -> e R (k m)", e=16, R=256)[
                                :, half * 128 + tl * 8:half * 128 + tl * 8 + 8, :]),
                        lambda y, row=row: y.dma_start(out=x1t_sb[:], in_=x1_d[row:row + 128, :]),
                    ], dma=True, waits={"y": yInter, "v": em.cnt["v"], "t": em.cnt["t"]})
                    po = pb3 if tl % 2 == 0 else pb4
                    tSc = E("t", [
                        (lambda t, k=k, hf=hf, tl=tl, po=po: t.matmul(
                            po[:, hf * 512:hf * 512 + 512],
                            pselT_sb[:, tl * 512 + k * 128:tl * 512 + k * 128 + 128],
                            scat_sb[:, k * 1024 + hf * 512:k * 1024 + hf * 512 + 512],
                            start=(k == 0), stop=(k == 3)))
                        for k in range(4) for hf in (0, 1)
                    ], waits={"y": ysc})
                    vF = E("v", [lambda v, po=po: v.tensor_add(outf_sb, x1t_sb[:], po[:, 0:1024])],
                           waits={"t": tSc, "y": ysc})
                    E("y", [lambda y, row=row: y.dma_start(out=out_h[row:row + 128, :], in_=outf_sb)],
                      dma=True, waits={"v": vF})

        _finalize_wait(em, E)
    return nc, dbg_outs


def _finalize_wait(em, E):
    E("t", [lambda t: t.nop()], waits=em.snap(), noinc=True)


def _emit_dbg_qkv(nc, em, E, L, dbg):
    outs = {}
    for nm in ("q_fm", "k_tm", "kb_tm", "v_tm", "k_fm", "xn_tm", "xn_fm"):
        outs[nm] = dbg("dbg_" + nm, [128, 4096])
    w = em.snap()
    E("y", [
        (lambda y, a=outs[nm], t=L[nm]: y.dma_start(out=a, in_=t[:, :]))
        for nm in outs
    ], dma=True, waits=w)


def _emit_dbg_solve(nc, em, E, L, dbg):
    outs = {}
    outs["W_fin0"] = dbg("dbg_W_fin0", [128, 1024])
    outs["A0"] = dbg("dbg_A0", [128, 1024])
    outs["B0"] = dbg("dbg_B0", [128, 1024])
    w = em.snap()
    E("y", [lambda y, a=outs["W_fin0"], t=L["W_fin"][0]: y.dma_start(out=a, in_=t[:, :]),
            lambda y, a=outs["A0"], t=L["A_bufs"][0]: y.dma_start(out=a, in_=t[:, :]),
            lambda y, a=outs["B0"], t=L["A_bufs"][1]: y.dma_start(out=a, in_=t[:, :])],
      dma=True, waits=w)


def _emit_dbg_chain(nc, em, E, L, dbg):
    outs = {}
    for nm in ("o_tm", "q_fm", "k_fm", "v_tm"):
        outs[nm] = dbg("dbg_" + nm, [128, 4096])
    for nm in ("H_sb", "M_sb"):
        outs[nm] = dbg("dbg_" + nm, [128, 1024])
    w = em.snap()
    E("y", [
        (lambda y, a=outs[nm], t=L[nm]: y.dma_start(out=a, in_=t[:, :]))
        for nm in outs
    ], dma=True, waits=w)



def _consts():
    ident = np.eye(128, dtype=np.float32)
    maskj = np.triu(np.ones((64, 64), np.float32))
    masklo = np.tril(np.ones((64, 64), np.float32), -1)
    maskup = np.triu(np.ones((64, 64), np.float32), 1)
    tie = np.tile(np.linspace(0.0, 1e-6, 16, dtype=np.float32), 8)[None, :]
    gmask = (np.arange(128)[:, None] // 16 == np.arange(8)[None, :]).astype(np.float32)
    return ident, maskj, masklo, maskup, tie, gmask


def kernel(L0_attn_x, L0_attn_Wq, L0_attn_Wk, L0_attn_Wv, L0_attn_Wo,
           L0_attn_alpha_raw, L0_attn_beta_raw,
           L0_ffn_controller, L0_ffn_f1, L0_ffn_f2, L0_ffn_bias):
    x = np.ascontiguousarray(np.asarray(L0_attn_x, np.float32))
    wqT = np.ascontiguousarray(np.asarray(L0_attn_Wq, np.float32).T)
    wkT = np.ascontiguousarray(np.asarray(L0_attn_Wk, np.float32).T)
    wvT = np.ascontiguousarray(np.asarray(L0_attn_Wv, np.float32).T)
    woT = np.ascontiguousarray(np.asarray(L0_attn_Wo, np.float32).T)
    ctrl = np.ascontiguousarray(np.asarray(L0_ffn_controller, np.float32).reshape(D, 64))
    f1w = np.ascontiguousarray(np.asarray(L0_ffn_f1, np.float32).reshape(D, 4096))
    f2w = np.ascontiguousarray(np.asarray(L0_ffn_f2, np.float32).reshape(4096, D))
    biasT = np.ascontiguousarray(np.asarray(L0_ffn_bias, np.float32).reshape(64, 64).T)
    araw = np.ascontiguousarray(np.asarray(L0_attn_alpha_raw, np.float32).reshape(1, 16))
    braw = np.ascontiguousarray(np.asarray(L0_attn_beta_raw, np.float32).reshape(1, 16))
    ident, maskj, masklo, maskup, tie, gmask = _consts()

    nc, _ = build_nc()
    in_maps = []
    for c in range(NCORE):
        xc = np.ascontiguousarray(x[2 * c:2 * c + 2].reshape(TC, D))
        in_maps.append(dict(
            x=xc, wqT=wqT, wkT=wkT, wvT=wvT, woT=woT, ctrl=ctrl, f1w=f1w,
            f2w=f2w, biasT=biasT, araw=araw, braw=braw, ident=ident,
            maskj=maskj, masklo=masklo, maskup=maskup, tie=tie, gmask=gmask))
    res = run_bass_kernel_spmd(nc, in_maps, core_ids=list(range(NCORE)))
    out = np.empty((16, 2048, D), np.float32)
    for c in range(NCORE):
        out[2 * c:2 * c + 2] = np.asarray(res.results[c]["out"], np.float32).reshape(2, 2048, D)
    return out
